# version 20
# baseline (speedup 1.0000x reference)
import numpy as np
import ml_dtypes
import concourse.bass as bass
import concourse.mybir as mybir
from concourse.bass_utils import run_bass_kernel_spmd

F32 = mybir.dt.float32
BF16 = mybir.dt.bfloat16
I32 = mybir.dt.int32
AF = mybir.ActivationFunctionType
ALU = mybir.AluOpType
AX = mybir.AxisListType

D = 1024
H = 16
DH = 64
MF = 256
RATIO = 0.0625
KEPS = 1e-4
C2H = 0.0625
LNEPS = 1e-5
B_EPS = RATIO * KEPS
T = 128
NCORES = 8


class _FakeInstr:
    def then_inc(self, *a, **k):
        return self


class _FakeEng:
    def __getattr__(self, name):
        def f(*a, **k):
            return _FakeInstr()
        return f


_NO_DRAIN = {"wait_ge", "drain", "nop", "memset", "dma_start", "then_inc",
             "collective_compute", "affine_select"}


class _DrainEng:

    def __init__(self, e):
        self._e = e

    def __getattr__(self, name):
        real = getattr(self._e, name)
        if name.startswith("_") or name in _NO_DRAIN or not callable(real):
            return real

        def f(*a, **k):
            self._e.drain()
            return real(*a, **k)
        return f


class Sched:

    def __init__(self, nc):
        self.nc = nc
        self.pass0 = True
        self.sems = {}
        self.counts = {}
        self.marks = {}

    def sem(self, name):
        if name not in self.sems:
            self.sems[name] = self.nc.alloc_semaphore(name)
        return self.sems[name]

    def begin(self, pass0):
        self.pass0 = pass0
        self.counts = {}

    def pub(self, name, instr, key=None, step=1):
        self.sem(name)
        n = self.counts.get(name, 0) + step
        self.counts[name] = n
        if key is not None:
            k = (name,) + (key if isinstance(key, tuple) else (key,))
            if self.pass0:
                self.marks[k] = n
            else:
                assert self.marks.get(k) == n, f"mark drift {k}: {self.marks.get(k)} vs {n}"
        if (not self.pass0):
            assert instr is not None and not isinstance(instr, _FakeInstr), f"null instr pub {name} {key}"
        if (not self.pass0) and instr is not None:
            instr.then_inc(self.sems[name], step)

    def wait(self, eng, name, key):
        if self.pass0:
            return
        k = (name,) + (key if isinstance(key, tuple) else (key,))
        v = self.marks.get(k)
        assert v is not None, f"missing mark {k}"
        eng.wait_ge(self.sems[name], v)


def build_program(nseq, ntiles):
    NT = nseq * ntiles
    S = ntiles * T
    ROWS = nseq * S

    nc = bass.Bass("TRN2", target_bir_lowering=False)

    x_hbm = nc.dram_tensor("x", [ROWS, D], F32, kind="ExternalInput").ap()
    wq_hbm = nc.dram_tensor("wq", [D, D], F32, kind="ExternalInput").ap()
    wk_hbm = nc.dram_tensor("wk", [D, D], F32, kind="ExternalInput").ap()
    wv_hbm = nc.dram_tensor("wv", [D, D], F32, kind="ExternalInput").ap()
    wo_hbm = nc.dram_tensor("wo", [D, D], F32, kind="ExternalInput").ap()
    bo_hbm = nc.dram_tensor("bo", [1, D], F32, kind="ExternalInput").ap()
    pjT_hbm = nc.dram_tensor("projT", [DH, MF], F32, kind="ExternalInput").ap()
    wfT_hbm = nc.dram_tensor("wfT", [D, D], F32, kind="ExternalInput").ap()
    out_hbm = nc.dram_tensor("out", [ROWS, D], F32, kind="ExternalOutput").ap()
    ccin = nc.dram_tensor("ccin", [1, 32], F32)
    ccout = nc.dram_tensor("ccout", [1, 32], F32)

    p0bf = nc.alloc_psum_tensor("p0bf", [128, 1024], BF16)
    pall = nc.alloc_psum_tensor("pall", [128, 3584], F32)
    assert nc.psum_banks_remaining == 0, nc.psum_banks_remaining
    W1 = pall[:, 0:1024]
    W2 = pall[:, 1024:2048]
    C0P = pall[:, 2048:3584]

    A = nc.alloc_sbuf_tensor
    idbf = A("idbf", [128, 128], BF16)
    idf32 = A("idf32", [128, 128], F32)
    onescol = A("onescol", [128, 1], BF16)
    onesrow = A("onesrow", [1, 128], F32)
    WA = A("WA", [128, 8 * D], BF16)
    WB = A("WB", [128, 8 * D], BF16)
    WC = A("WC", [128, 8 * D], BF16)
    projTb = A("projTb", [128, MF], BF16)
    bo_row = A("bo_row", [1, D], F32)
    wsbc = A("wsbc", [128, 1], F32)
    wsd127 = A("wsd127", [128, 1], F32)
    iws = A("iws", [128, 1], F32)
    absacc = A("absacc", [128, 8], F32)
    abssum = A("abssum", [128, 1], F32)
    wsrow = A("wsrow", [1, 1], F32)

    xb = [A("xb0", [128, D], F32), A("xb1", [128, D], F32), A("xb2", [128, D], F32)]
    bnst = [A("bns0", [128, 12], F32), A("bns1", [128, 12], F32)]
    mv = [A("mv0", [128, 2], F32), A("mv1", [128, 2], F32)]
    rstd = [A("rstd0", [128, 1], F32), A("rstd1", [128, 1], F32)]
    bbv = [A("bb0", [128, 1], F32), A("bb1", [128, 1], F32)]
    bnst2 = [A("bns20", [128, 12], F32), A("bns21", [128, 12], F32)]
    mv2 = [A("mv20", [128, 2], F32), A("mv21", [128, 2], F32)]
    rstd2 = [A("rstd20", [128, 1], F32), A("rstd21", [128, 1], F32)]
    bbv2 = [A("bb20", [128, 1], F32), A("bb21", [128, 1], F32)]
    xlnbf = [A("xlnbf0", [128, D], BF16), A("xlnbf1", [128, D], BF16),
             A("xlnbf2", [128, D], BF16)]
    xlnT = A("xlnT", [128, D], BF16)
    kqbf = A("kqbf", [128, D], BF16)
    kqT2 = [A("kqT0", [128, D], BF16), A("kqT1", [128, D], BF16)]
    scrK = A("scrK", [128, D], F32)
    sqjunk = A("sqjunk", [128, D], F32)
    diag = A("diag", [128, H], F32)
    akbuf = A("akbuf", [128, H], F32)
    expbuf = A("expbuf", [128, 4 * D], BF16)
    v1buf2 = [A("v1buf0", [128, H * 65], BF16), A("v1buf1", [128, H * 65], BF16)]
    gmax128 = A("gmax128", [128, 1], F32)
    redb = A("redb", [128, 1], F32)
    cx0 = A("cx0", [128, nseq * 2080], F32)
    sxl = A("sxl", [1, nseq * D], F32)
    Mloc = A("Mloc", [1, 1], F32)
    mrow = A("mrow", [1, 32], F32)
    Msc = A("Msc", [1, 32], F32)
    eM128 = A("eM128", [128, 1], F32)
    sxlT = A("sxlT", [128, 8], BF16)
    rows4 = A("rows4", [1, nseq * D + 1408], F32)
    ctxaug = A("ctxaug", [128, 2080], BF16)
    bscol = A("bscol", [128, 1365], F32)
    t1b = A("t1b", [128, H * 65], F32)
    rowmax = A("rowmax", [128, H], F32)
    mvsave = A("mvsave", [128, 2 * ntiles * nseq], F32)
    drb = A("drb", [128, H], F32)
    aexp = A("aexp", [128, H], F32)
    anb = A("anb", [128, H], F32)
    dinv = A("dinv", [128, H], F32)
    attnbf = A("attnbf", [128, D], BF16)
    xln2f = A("xln2f", [128, D], F32)
    amaxb = A("amaxb", [128, 1], F32)
    am8 = A("am8", [128, 1], F32)
    rzb = A("rzb", [128, 1], F32)
    s2b = A("s2b", [128, 1], F32)
    aqi = A("aqi", [128, D], I32)
    aqbf = A("aqbf", [128, D], BF16)
    tmp1 = A("tmp1", [128, 1], F32)
    nmr = A("nmr", [128, 1], F32)
    lnepsb = A("lnepsb", [128, 1], F32)

    sch = Sched(nc)
    P = sch

    def seqof(g):
        return g // ntiles

    def ttof(g):
        return g % ntiles

    C0_PIECES = [(0, 7), (7, 7), (14, 2)]

    def c0_slot_psum(s):
        return 2048 + (s // 7) * 512 + (s % 7) * 65

    def cx0_seq(s):
        return cx0[:, s * 2080:(s + 1) * 2080]

    GC_PIECES = [(0, 7), (7, 7), (14, 2)]

    def expcol(h):
        return 1024 * ((h % 8) // 2) + 256 * (h % 2) + 512 * (h // 8)

    def kqcol(h):
        return 128 * (h % 8)

    def kqrows(h):
        return (0, 64) if h < 8 else (64, 128)

    def gc_col(h):
        return 2048 + (h // 7) * 512 + (h % 7) * 65

    def bs_col(h):
        return (h // 7) * 455 + (h % 7) * 65


    def stream_sync(e):
        for c in range(8):
            if c >= 2:
                P.wait(e, 'dve', ('ffn1', c - 2))
            ins = e.dma_start(out=xb[c % 2][:], in_=wfT_hbm[128 * c:128 * (c + 1), :])
            P.pub('dmax%d' % (c % 2), ins, ('wf1', c), 16)
        for c in range(8):
            if c < 2:
                P.wait(e, 'dve', ('ffn1', 6 + c))
            else:
                P.wait(e, 'dve', ('ffn2', c - 2))
            ins = e.dma_start(out=xb[c % 2][:], in_=wfT_hbm[128 * c:128 * (c + 1), :])
            P.pub('dmax%d' % (c % 2), ins, ('wf2', c), 16)
        ins = e.dma_start(out=bo_row[:], in_=bo_hbm[:, :])
        P.pub('dmabo', ins, 'bo', 16)
        for g in range(NT):
            if g >= 3:
                P.wait(e, 'act', ('a1xln', g - 3))
                P.wait(e, 'dve', ('v1bn', g - 3))
            elif g < 2:
                P.wait(e, 'dve', ('ffn2', 6 + g))
            r0 = seqof(g) * S + ttof(g) * T
            ins = e.dma_start(out=xb[g % 3][:], in_=x_hbm[r0:r0 + T, :])
            P.pub('dmax%d' % (g % 3), ins, ('x1', g), 16)
        for g in range(NT):
            if g >= 3:
                P.wait(e, 'dmao%d' % ((g - 3) % 3), ('o', g - 3))
            else:
                P.wait(e, 'act', ('a1xln', NT - 1))
                P.wait(e, 'dve', ('v1bn', NT - 1))
                P.wait(e, 'dve', ('v1sx', NT - 1))
            r0 = seqof(g) * S + ttof(g) * T
            ins = e.dma_start(out=xb[g % 3][:], in_=x_hbm[r0:r0 + T, :])
            P.pub('dmax%d' % (g % 3), ins, ('x2', g), 16)

    def stream_gp(e):
        e.memset(idbf[:], 0.0).then_inc(P.sem('gp'), 1)
        e.memset(idf32[:], 0.0).then_inc(P.sem('gp'), 1)
        e.memset(onescol[:], 1.0).then_inc(P.sem('gp'), 1)
        e.memset(lnepsb[:], LNEPS).then_inc(P.sem('gp'), 1)
        e.memset(onesrow[:], 1.0).then_inc(P.sem('gp'), 1)
        P.counts['gp'] = P.counts.get('gp', 0) + 5
        if not P.pass0:
            e.wait_ge(P.sem('gp'), 5)
        e.affine_select(out=idbf[:], in_=idbf[:], compare_op=ALU.not_equal,
                        fill=1.0, base=0, pattern=[[-1, 128]], channel_multiplier=1)
        ins = e.affine_select(out=idf32[:], in_=idf32[:], compare_op=ALU.not_equal,
                              fill=1.0, base=0, pattern=[[-1, 128]], channel_multiplier=1)
        P.pub('gp', ins, 'ident')
        for c in range(8):
            ins = e.dma_start(out=WA[:, D * c:D * (c + 1)], in_=wk_hbm[128 * c:128 * (c + 1), :])
            P.pub('dmawk', ins, ('wk', c), 16)
        for c in range(8):
            ins = e.dma_start(out=WB[:, D * c:D * (c + 1)], in_=wv_hbm[128 * c:128 * (c + 1), :])
            P.pub('dmawv', ins, ('wv', c), 16)
        e.dma_start(out=projTb[0:64, :], in_=pjT_hbm[:, :]).then_inc(P.sem('dmapj'), 16)
        P.counts['dmapj'] = P.counts.get('dmapj', 0) + 16
        ins = e.dma_start(out=projTb[64:128, :], in_=pjT_hbm[:, :])
        P.pub('dmapj', ins, 'pj', 16)
        P.wait(e, 'dve', 'mrow')
        ins = e.dma_start(out=ccin.ap(), in_=mrow[:])
        P.pub('dmacc', ins, 'ccin', 16)
        P.wait(e, 'dmacc', 'ccin')
        ins = e.collective_compute(
            "AllReduce", ALU.max,
            replica_groups=[list(range(NCORES))],
            ins=[ccin.ap().opt()],
            outs=[ccout.ap().opt()],
        )
        P.pub('cc', ins, 'ar', 1)
        P.wait(e, 'cc', 'ar')
        ins = e.dma_start(out=Msc[:], in_=ccout.ap())
        P.pub('dmamsc', ins, 'msc', 16)
        P.wait(e, 'pe', ('p1k', NT - 1))
        P.wait(e, 'pe', ('sumv', nseq - 1))
        for c in range(8):
            ins = e.dma_start(out=WA[:, D * c:D * (c + 1)], in_=wq_hbm[128 * c:128 * (c + 1), :])
            P.pub('dmawq', ins, ('wq', c), 16)
        for c in range(8):
            ins = e.dma_start(out=WB[:, D * c:D * (c + 1)], in_=wo_hbm[128 * c:128 * (c + 1), :])
            P.pub('dmawo', ins, ('wo', c), 16)

    def ln_stats(e, g, tag, sqtag, bn, mvb, skip_sum=False):
        par = g % 2
        xbuf = xb[g % 3]
        if not skip_sum:
            e.tensor_reduce(out=bn[par][:, 2:3], in_=xbuf[:], axis=AX.X, op=ALU.add)
        P.wait(e, 'act', (sqtag, g))
        e.tensor_scalar(out=mvb[par][:, 0:1], in0=bn[par][:, 2:3], scalar1=-1.0 / D,
                        scalar2=None, op0=ALU.mult)
        e.tensor_scalar(out=bn[par][:, 1:2], in0=mvb[par][:, 0:1],
                        scalar1=mvb[par][:, 0:1], scalar2=None, op0=ALU.mult)
        ins = e.tensor_scalar(out=mvb[par][:, 1:2], in0=bn[par][:, 0:1],
                              scalar1=1.0 / D, scalar2=bn[par][:, 1:2],
                              op0=ALU.mult, op1=ALU.subtract)
        P.pub('dve', ins, (tag, g))

    def ln_apply(e, g, tag, bntag, mvb, rs, outbuf, outslice=None):
        par = g % 2
        P.wait(e, 'dve', (bntag, g))
        e.activation(out=tmp1[:], in_=mvb[par][:, 1:2], func=AF.Ln, bias=lnepsb[:])
        e.activation(out=rs[par][:], in_=tmp1[:], func=AF.Exp, scale=-0.5)
        e.activation(out=nmr[:], in_=mvb[par][:, 0:1], func=AF.Identity,
                     scale=rs[par][:])
        ins = e.activation(out=outbuf[:] if outslice is None else outslice,
                           in_=xb[g % 3][:], func=AF.Identity, scale=rs[par][:],
                           bias=nmr[:])
        return ins

    def stream_dve(e):
        for c in range(8):
            P.wait(e, 'dmax%d' % (c % 2), ('wf1', c))
            ins = e.tensor_reduce(out=absacc[:, c:c + 1], in_=xb[c % 2][:],
                                  axis=AX.X, op=ALU.add, apply_absolute_value=True)
            P.pub('dve', ins, ('ffn1', c))
        ins = e.tensor_reduce(out=abssum[:], in_=absacc[:], axis=AX.X, op=ALU.add)
        P.pub('dve', ins, 'abssum')
        P.wait(e, 'pe', 'wsT')
        e.tensor_reduce(out=wsrow[:], in_=pall[0:1, 0:128], axis=AX.X, op=ALU.add)
        ins = e.tensor_scalar(out=wsrow[:], in0=wsrow[:], scalar1=1.0 / (D * D),
                              scalar2=1e-8, op0=ALU.mult, op1=ALU.add)
        P.pub('dve', ins, 'wsrow')
        P.wait(e, 'pe', 'wsbc')
        e.tensor_copy(wsbc[:], pall[:, 512:513])
        e.tensor_scalar(out=wsd127[:], in0=wsbc[:], scalar1=1.0 / 127.0,
                        scalar2=None, op0=ALU.mult)
        ins = e.reciprocal(out=iws[:], in_=wsbc[:])
        P.pub('dve', ins, 'iws')
        P.wait(e, 'dmapj', 'pj')
        ins = e.tensor_scalar(out=projTb[:], in0=projTb[:], scalar1=DH ** -0.25,
                              scalar2=None, op0=ALU.mult)
        P.pub('dve', ins, 'pjscale')
        for c in range(8):
            P.wait(e, 'dmax%d' % (c % 2), ('wf2', c))
            e.tensor_scalar(out=scrK[:], in0=xb[c % 2][:], scalar1=iws[:],
                            scalar2=None, op0=ALU.mult)
            e.tensor_scalar(out=scrK[:], in0=scrK[:], scalar1=12582912.0,
                            scalar2=-12582912.0, op0=ALU.add, op1=ALU.add)
            e.tensor_copy(aqi[:], scrK[:])
            ins = e.tensor_scalar(out=WC[:, D * c:D * (c + 1)], in0=aqi[:],
                                  scalar1=1.0, scalar2=-1.0, op0=ALU.min, op1=ALU.max)
            P.pub('dve', ins, ('ffn2', c))
        for g in range(-1, NT + 1):
            if 0 <= g + 1 < NT:
                gn = g + 1
                P.wait(e, 'dmax%d' % (gn % 3), ('x1', gn))
                if gn >= 2:
                    P.wait(e, 'act', ('a1xln', gn - 2))
                ln_stats(e, gn, 'v1bn', 'a1sq', bnst, mv)
                ins = e.tensor_copy(mvsave[:, 2 * gn:2 * gn + 2], mv[gn % 2][:, 0:2])
                P.pub('dve', ins, ('v1mvs', gn))
            if 0 <= g < NT:
                par = g % 2
                P.wait(e, 'act', ('a1ksq', g))
                ins = e.tensor_copy(kqbf[:], W1[:])
                P.pub('dve', ins, ('v1ke', g))
                ins = e.tensor_reduce(out=diag[:], in_=scrK[:].rearrange("p (h d) -> p h d", h=H),
                                      axis=AX.X, op=ALU.add)
                P.pub('dve', ins, ('v1diag', g))
                P.wait(e, 'pe', ('p1v', g))
                P.wait(e, 'act', ('a1ak', g))
                v3 = W2[:].rearrange("p (h d) -> p h d", h=H)
                akb = akbuf[:].unsqueeze(2).broadcast_to([128, H, DH])
                dst = v1buf2[par][:].rearrange("p (h c) -> p h c", h=H)[:, :, 0:DH]
                e.tensor_tensor(out=dst, in0=v3, in1=akb, op=ALU.mult)
                ins = e.tensor_copy(v1buf2[par][:].rearrange("p (h c) -> p h c", h=H)[:, :, 64:65],
                                    akbuf[:].unsqueeze(2))
                P.pub('dve', ins, ('v1v1', g))
            if 1 <= g <= NT:
                h = g - 1
                sqh, tth = seqof(h), ttof(h)
                for b in range(3):
                    P.wait(e, 'act', ('a1ek%d' % b, h))
                    e.tensor_reduce(out=redb[:], in_=expbuf[:, 1024 * b:1024 * (b + 1)],
                                    axis=AX.X, op=ALU.max)
                    if h == 0 and b == 0:
                        ins = e.tensor_copy(gmax128[:], redb[:])
                    else:
                        ins = e.tensor_tensor(out=gmax128[:], in0=gmax128[:], in1=redb[:],
                                              op=ALU.max)
                    P.pub('dve', ins, ('v1gm%d' % b, h))
                for half, nm in ((0, 'a'), (1, 'b')):
                    P.wait(e, 'pe', ('p1c0%s' % nm.upper(), h))
                    base = sqh * 2080 + half * 1040
                    ins = None
                    for (s0, ns) in C0_PIECES:
                        srcp = pall[:, c0_slot_psum(s0):c0_slot_psum(s0) + ns * 65]
                        dstp = cx0[:, base + s0 * 65:base + (s0 + ns) * 65]
                        if tth == 0:
                            ins = e.tensor_copy(dstp, srcp)
                        else:
                            ins = e.tensor_tensor(out=dstp, in0=dstp, in1=srcp, op=ALU.add)
                    P.pub('dve', ins, ('v1c0%s' % nm, h))
                P.wait(e, 'pe', ('p1sx', h))
                srcp = pall[0:1, 0:1024]
                dstp = sxl[0:1, sqh * D:(sqh + 1) * D]
                if tth == 0:
                    ins = e.tensor_copy(dstp, srcp)
                else:
                    ins = e.tensor_tensor(out=dstp, in0=dstp, in1=srcp, op=ALU.add)
                P.pub('dve', ins, ('v1sx', h))
                P.wait(e, 'act', ('a1ek3', h))
                e.tensor_reduce(out=redb[:], in_=expbuf[:, 3072:4096],
                                axis=AX.X, op=ALU.max)
                ins = e.tensor_tensor(out=gmax128[:], in0=gmax128[:], in1=redb[:],
                                      op=ALU.max)
                P.pub('dve', ins, ('v1gm3', h))
        P.wait(e, 'pe', 'gmaxT')
        ins = e.tensor_reduce(out=Mloc[:], in_=pall[0:1, 0:128], axis=AX.X, op=ALU.max)
        P.pub('dve', ins, 'gmaxfin')
        ins = e.tensor_copy(mrow[:], Mloc[:].broadcast_to([1, 32]))
        P.pub('dve', ins, 'mrow')
        P.wait(e, 'pe', 'mbc')
        ins = e.reciprocal(out=eM128[:], in_=pall[:, 512:513])
        P.pub('dve', ins, 'aeM')
        for s in range(nseq):
            P.wait(e, 'pe', ('sxlT', s))
            ins = e.tensor_copy(sxlT[:], pall[:, 0:8])
            P.pub('dve', ins, ('vsxlT', s))
            P.wait(e, 'pe', ('sumv', s))
            ins = e.tensor_copy(rows4[0:1, s * D:(s + 1) * D], pall[0:1, 1024:2048])
            P.pub('dve', ins, ('vsumv', s))
        for g in range(-1, NT + 2):
            if 1 <= g <= NT:
                h = g - 1
                parh = h % 2
                P.wait(e, 'pe', ('p2y', h))
                xh = xb[h % 3]
                ins = e.tensor_tensor(out=xh[:], in0=xh[:], in1=W1[:], op=ALU.add)
                P.pub('dve', ins, ('v2x1', h))
            if 2 <= g <= NT + 1:
                hm = g - 2
                P.wait(e, 'pe', ('p2y2', hm))
                xm = xb[hm % 3]
                ins = e.scalar_tensor_tensor(out=xm[:], in0=W2[:], scalar=s2b[:],
                                             in1=xm[:], op0=ALU.mult, op1=ALU.add)
                P.pub('dve', ins, ('v2out', hm))
            if 0 <= g < NT and ttof(g) == 0:
                sq = seqof(g)
                P.wait(e, 'dve', 'aeM')
                for grp in range(2):
                    P.wait(e, 'pe', ('ub', sq, grp))
                    last = None
                    for (s0, ns) in C0_PIECES:
                        ubp = pall[:, (s0 // 7) * 512 + (s0 % 7) * 65:
                                   (s0 // 7) * 512 + (s0 % 7) * 65 + ns * 65]
                        ubv = ubp.rearrange("p (s c) -> p s c", c=65)[:, :, 0:DH]
                        t1v = t1b[:, s0 * 65:(s0 + ns) * 65].rearrange("p (s c) -> p s c", c=65)[:, :, 0:DH]
                        e.tensor_scalar(out=t1v, in0=ubv, scalar1=KEPS, scalar2=None, op0=ALU.mult)
                        cb = sq * 2080 + grp * 1040
                        cxv = cx0[:, cb + s0 * 65:cb + (s0 + ns) * 65].rearrange("p (s c) -> p s c", c=65)[:, :, 0:DH]
                        ctv = ctxaug[:, grp * 1040 + s0 * 65:grp * 1040 + (s0 + ns) * 65].rearrange("p (s c) -> p s c", c=65)[:, :, 0:DH]
                        last = e.scalar_tensor_tensor(out=ctv, in0=cxv, scalar=eM128[:],
                                                      in1=t1v, op0=ALU.mult, op1=ALU.add)
                    P.pub('dve', last, ('vctx%d' % grp, sq))
                cx64 = cx0_seq(sq)[:].rearrange("p (s c) -> p s c", c=65)[:, :, 64:65]
                ct64 = ctxaug[:].rearrange("p (s c) -> p s c", c=65)[:, :, 64:65]
                ins = e.tensor_scalar(out=ct64, in0=cx64, scalar1=eM128[:],
                                      scalar2=KEPS * float(S), op0=ALU.mult, op1=ALU.add)
                P.pub('dve', ins, ('vctx', sq))
                P.wait(e, 'pe', ('scol', sq))
                bse = nseq * D
                ins = None
                for (d0, a0, b0, n) in ((0, 0, 520, 455), (455, 455, 975, 65),
                                        (520, 1040, 1560, 390), (910, 1430, 1950, 130)):
                    e.tensor_copy(rows4[0:1, bse + d0:bse + d0 + n], pall[0:1, a0:a0 + n])
                    ins = e.tensor_tensor(out=rows4[0:1, bse + d0:bse + d0 + n],
                                          in0=rows4[0:1, bse + d0:bse + d0 + n],
                                          in1=pall[0:1, b0:b0 + n], op=ALU.add)
                P.pub('dve', ins, ('vscol', sq))
            if 0 <= g < NT:
                par = g % 2
                P.wait(e, 'act', ('a2qsq', g))
                ins = e.tensor_reduce(out=diag[:], in_=scrK[:].rearrange("p (h d) -> p h d", h=H),
                                      axis=AX.X, op=ALU.add)
                P.pub('dve', ins, ('v2diagq', g))
            if 0 <= g < NT:
                par = g % 2
                for b in range(4):
                    P.wait(e, 'pe', ('p2ddtm%d' % b, g))
                    win = (W1 if b % 2 == 0 else W2)
                    ins = e.tensor_reduce(out=rowmax[:, 4 * b:4 * b + 4],
                                          in_=win[:].rearrange("p (h m) -> p h m", h=4),
                                          axis=AX.X, op=ALU.max)
                    P.pub('dve', ins, ('v2rm%d' % b, g))
                rmv = rowmax[:].rearrange("p (b q) -> p b q", q=4)
                e.scalar_tensor_tensor(out=drb[:, 0:8], in0=diag[:, 0:8], scalar=C2H,
                                       in1=rmv[:, :, 0:2], op0=ALU.mult, op1=ALU.add)
                ins = e.scalar_tensor_tensor(out=drb[:, 8:16], in0=diag[:, 8:16], scalar=C2H,
                                             in1=rmv[:, :, 2:4], op0=ALU.mult, op1=ALU.add)
                P.pub('dve', ins, ('v2dr', g))
            if 1 <= g <= NT:
                h = g - 1
                parh = h % 2
                if h >= 2:
                    P.wait(e, 'act', ('a2xln2', h - 2))
                ln_stats(e, h, 'v2lbn', 'a2lsq', bnst2, mv2, skip_sum=True)
            if 0 <= g < NT:
                P.wait(e, 'act', ('a2aexp', g))
                ins = e.tensor_scalar(out=anb[:], in0=aexp[:], scalar1=RATIO,
                                      scalar2=None, op0=ALU.mult)
                P.pub('dve', ins, ('v2an', g))
            if 0 <= g < NT:
                sq = seqof(g)
                P.wait(e, 'act', ('abscol', sq))
                for (s0, ns) in GC_PIECES:
                    P.wait(e, 'pe', ('p2gc%d' % s0, g))
                    gc3 = pall[:, gc_col(s0):gc_col(s0) + ns * 65].rearrange("p (s c) -> p s c", c=65)
                    an3 = anb[:, s0:s0 + ns].unsqueeze(2).broadcast_to([128, ns, 65])
                    t1v = t1b[:, s0 * 65:(s0 + ns) * 65].rearrange("p (s c) -> p s c", c=65)
                    e.tensor_tensor(out=t1v, in0=gc3, in1=an3, op=ALU.mult)
                    bsv = bscol[:, bs_col(s0):bs_col(s0) + ns * 65].rearrange("p (s c) -> p s c", c=65)
                    last = e.tensor_tensor(out=t1v, in0=t1v, in1=bsv, op=ALU.add)
                    if s0 == 7:
                        e.reciprocal(out=dinv[:, 0:8],
                                     in_=t1b[:, 0:8 * 65].rearrange("p (h c) -> p h c", c=65)[:, :, 64:65].rearrange("p h c -> p (h c)"))
                        t1n = t1b[:, 0:8 * 65].rearrange("p (h c) -> p h c", c=65)[:, :, 0:DH]
                        dv3 = dinv[:, 0:8].unsqueeze(2).broadcast_to([128, 8, DH])
                        ins = e.tensor_tensor(out=attnbf[:, 0:8 * DH].rearrange("p (h d) -> p h d", h=8),
                                              in0=t1n, in1=dv3, op=ALU.mult)
                        P.pub('dve', ins, ('v2attnA', g))
                P.pub('dve', last, ('v2t1', g))
                e.reciprocal(out=dinv[:, 8:16],
                             in_=t1b[:, 8 * 65:16 * 65].rearrange("p (h c) -> p h c", c=65)[:, :, 64:65].rearrange("p h c -> p (h c)"))
                t1n = t1b[:, 8 * 65:16 * 65].rearrange("p (h c) -> p h c", c=65)[:, :, 0:DH]
                dv3 = dinv[:, 8:16].unsqueeze(2).broadcast_to([128, 8, DH])
                ins = e.tensor_tensor(out=attnbf[:, 8 * DH:16 * DH].rearrange("p (h d) -> p h d", h=8),
                                      in0=t1n, in1=dv3, op=ALU.mult)
                P.pub('dve', ins, ('v2attn', g))
                P.wait(e, 'pe', ('p2attnT', g))
                ins = e.tensor_copy(xlnT[:], p0bf[:])
                P.pub('dve', ins, ('v2attnTe', g))
            if 1 <= g <= NT:
                h = g - 1
                P.wait(e, 'act', ('a2xln2', h))
                if h >= 1:
                    P.wait(e, 'pe', ('p2aqT', h - 1))
                e.tensor_reduce(out=amaxb[:], in_=xln2f[:], axis=AX.X, op=ALU.max,
                                apply_absolute_value=True)
                e.tensor_scalar(out=am8[:], in0=amaxb[:], scalar1=1e-8, scalar2=None, op0=ALU.add)
                e.reciprocal(out=rzb[:], in_=am8[:])
                e.tensor_scalar(out=s2b[:], in0=am8[:], scalar1=wsd127[:], scalar2=None, op0=ALU.mult)
                e.tensor_scalar(out=xln2f[:], in0=xln2f[:], scalar1=rzb[:], scalar2=127.0,
                                op0=ALU.mult, op1=ALU.mult)
                e.tensor_scalar(out=xln2f[:], in0=xln2f[:], scalar1=12582912.0,
                                scalar2=-12582912.0, op0=ALU.add, op1=ALU.add)
                e.tensor_copy(aqi[:], xln2f[:])
                ins = e.tensor_copy(aqbf[:], aqi[:])
                P.pub('dve', ins, ('v2aq', h))
            if 1 <= g <= NT:
                h = g - 1
                P.wait(e, 'pe', ('p2aqT', h))
                ins = e.tensor_copy(xlnT[:], p0bf[:])
                P.pub('dve', ins, ('v2aqTe', h))

    def stream_act(e):
        for g in range(-1, NT + 1):
            if 0 <= g < NT:
                P.wait(e, 'pe', ('p1xlnT', g))
                ins = e.activation(out=xlnT[:], in_=p0bf[:], func=AF.Identity)
                P.pub('act', ins, ('v1xlnTe', g))
            if 0 <= g + 1 < NT:
                gn = g + 1
                parn = gn % 2
                P.wait(e, 'dmax%d' % (gn % 3), ('x1', gn))
                if gn == 0:
                    P.wait(e, 'dve', ('ffn2', 7))
                ins = e.activation(out=sqjunk[:], in_=xb[gn % 3][:], func=AF.Square,
                                   accum_out=bnst[parn][:, 0:1])
                P.pub('act', ins, ('a1sq', gn))
                if gn >= 3:
                    P.wait(e, 'pe', ('p1sx', gn - 3))
                ins = ln_apply(e, gn, 'a1xln', 'v1bn', mv, rstd, xlnbf[gn % 3])
                P.pub('act', ins, ('a1xln', gn))
            if 0 <= g < NT:
                P.wait(e, 'pe', ('p1k', g))
                if g >= 1:
                    P.wait(e, 'dve', ('v1diag', g - 1))
                ins = e.activation(out=scrK[:], in_=W1[:], func=AF.Square)
                P.pub('act', ins, ('a1ksq', g))
                P.wait(e, 'pe', ('p1kT', g))
                ins = e.activation(out=kqT2[g % 2][:], in_=p0bf[:], func=AF.Identity)
                P.pub('act', ins, ('v1kTe', g))
                P.wait(e, 'dve', ('v1diag', g))
                ins = e.activation(out=akbuf[:], in_=diag[:], func=AF.Exp, scale=-C2H)
                P.pub('act', ins, ('a1ak', g))
            if 1 <= g <= NT:
                h = g - 1
                for b in range(4):
                    P.wait(e, 'pe', ('p1dd%d' % b, h))
                    if h >= 1:
                        P.wait(e, 'dve', ('v1gm%d' % b, h - 1))
                    win = (W1 if b % 2 == 0 else W2)
                    ins = e.activation(out=expbuf[:, 1024 * b:1024 * (b + 1)], in_=win[:],
                                       func=AF.Exp)
                    P.pub('act', ins, ('a1ek%d' % b, h))
        for g in range(-1, NT + 2):
            if 0 <= g < NT and ttof(g) == 0:
                sq = seqof(g)
                P.wait(e, 'pe', ('bscolbc', sq))
                ins = e.activation(out=bscol[:].rearrange("p (b q) -> p b q", q=455),
                                   in_=pall[:, 0:1536].rearrange("p (b q) -> p b q", q=512)[:, :, 0:455],
                                   func=AF.Identity, scale=B_EPS)
                P.pub('act', ins, ('abscol', sq))
            if 0 <= g < NT:
                P.wait(e, 'pe', ('p2xlnT', g))
                ins = e.activation(out=xlnT[:], in_=p0bf[:], func=AF.Identity)
                P.pub('act', ins, ('v2xlnTe', g))
            if 2 <= g <= NT + 1:
                hm = g - 2
                P.wait(e, 'dve', ('v2out', hm))
                r0 = seqof(hm) * S + ttof(hm) * T
                ins = e.dma_start(out=out_hbm[r0:r0 + T, :], in_=xb[hm % 3][:])
                P.pub('dmao%d' % (hm % 3), ins, ('o', hm), 16)
            if 0 <= g < NT:
                P.wait(e, 'pe', ('p2q', g))
                ins = e.activation(out=kqbf[:], in_=W1[:], func=AF.Identity)
                P.pub('act', ins, ('v2qe', g))
                if g >= 1:
                    P.wait(e, 'dve', ('v2diagq', g - 1))
                ins = e.activation(out=scrK[:], in_=W1[:], func=AF.Square)
                P.pub('act', ins, ('a2qsq', g))
                P.wait(e, 'pe', ('p2qT', g))
                ins = e.activation(out=kqT2[g % 2][:], in_=p0bf[:], func=AF.Identity)
                P.pub('act', ins, ('v2qTe', g))
            if 1 <= g <= NT:
                h = g - 1
                parh = h % 2
                P.wait(e, 'dve', ('v2x1', h))
                P.wait(e, 'dve', ('v2diagq', h))
                e.activation(out=sqjunk[:], in_=xb[h % 3][:], func=AF.Identity,
                             accum_out=bnst2[parh][:, 2:3])
                ins = e.activation(out=sqjunk[:], in_=xb[h % 3][:], func=AF.Square,
                                   accum_out=bnst2[parh][:, 0:1])
                P.pub('act', ins, ('a2lsq', h))
            if 0 <= g < NT:
                for b in range(8):
                    P.wait(e, 'pe', ('p2ddmm%d' % b, g))
                    win = pall[:, 2048 + 512 * (b % 3):2048 + 512 * (b % 3) + 512]
                    ins = e.activation(out=expbuf[:, 512 * b:512 * (b + 1)], in_=win,
                                       func=AF.Exp)
                    P.pub('act', ins, ('a2g%d' % b, g))
                    if b == 5:
                        P.wait(e, 'dve', ('v2dr', g))
                        ins = e.activation(out=aexp[:], in_=drb[:], func=AF.Exp, scale=-1.0)
                        P.pub('act', ins, ('a2aexp', g))
            if 1 <= g <= NT:
                h = g - 1
                parh = h % 2
                if h >= 1:
                    P.wait(e, 'dve', ('v2aq', h - 1))
                P.wait(e, 'dve', ('v2lbn', h))
                e.activation(out=tmp1[:], in_=mv2[parh][:, 1:2], func=AF.Ln, bias=lnepsb[:])
                e.activation(out=rstd2[parh][:], in_=tmp1[:], func=AF.Exp, scale=-0.5)
                e.activation(out=nmr[:], in_=mv2[parh][:, 0:1], func=AF.Identity,
                             scale=rstd2[parh][:])
                ins = e.activation(out=xln2f[:], in_=xb[h % 3][:], func=AF.Identity,
                                   scale=rstd2[parh][:], bias=nmr[:])
                P.pub('act', ins, ('a2xln2', h))
            if 0 <= g + 1 < NT:
                gn = g + 1
                parn = gn % 2
                P.wait(e, 'dmax%d' % (gn % 3), ('x2', gn))
                if gn >= 2:
                    P.wait(e, 'pe', ('p2xlnT', gn - 2))
                P.wait(e, 'dve', ('v1mvs', gn))
                e.activation(out=tmp1[:], in_=mvsave[:, 2 * gn + 1:2 * gn + 2],
                             func=AF.Ln, bias=lnepsb[:])
                e.activation(out=rstd[parn][:], in_=tmp1[:], func=AF.Exp, scale=-0.5)
                e.activation(out=nmr[:], in_=mvsave[:, 2 * gn:2 * gn + 1],
                             func=AF.Identity, scale=rstd[parn][:])
                ins = e.activation(out=xlnbf[gn % 3][:], in_=xb[gn % 3][:],
                                   func=AF.Identity, scale=rstd[parn][:], bias=nmr[:])
                P.pub('act', ins, ('a2xln', gn))
        if not P.pass0:
            for pp in range(3):
                cnt = len([x for x in range(NT) if x % 3 == pp])
                e.wait_ge(P.sem('dmao%d' % pp), 16 * cnt)

    def stream_pe(e):
        P.wait(e, 'gp', 'ident')
        P.wait(e, 'dve', 'abssum')
        ins = e.transpose(out=pall[0:1, 0:128], in_=abssum[:], identity=idf32[:])
        P.pub('pe', ins, 'wsT')
        P.wait(e, 'dve', 'wsrow')
        ins = e.matmul(pall[:, 512:513], onesrow[:], wsrow[:], start=True, stop=True)
        P.pub('pe', ins, 'wsbc')
        for g in range(NT + 1):
            par = g % 2
            if g < NT:
                P.wait(e, 'act', ('a1xln', g))
                if g >= 1:
                    P.wait(e, 'act', ('v1kTe', g - 1))
                ins = None
                for c in range(8):
                    ins = e.transpose(out=p0bf[:, 128 * c:128 * (c + 1)],
                                      in_=xlnbf[g % 3][:, 128 * c:128 * (c + 1)], identity=idbf[:])
                P.pub('pe', ins, ('p1xlnT', g))
                P.wait(e, 'act', ('v1xlnTe', g))
                if g >= 2:
                    P.wait(e, 'act', ('a1ek2', g - 2))
                    P.wait(e, 'dve', ('v1sx', g - 2))
                elif g == 1:
                    P.wait(e, 'dve', ('v1ke', 0))
                if g == 0:
                    P.wait(e, 'dmawk', ('wk', 7))
                ins = None
                for j in range(2):
                    for c in range(8):
                        ins = e.matmul(W1[:, 512 * j:512 * (j + 1)],
                                       xlnT[:, 128 * c:128 * (c + 1)],
                                       WA[:, D * c + 512 * j:D * c + 512 * (j + 1)],
                                       start=(c == 0), stop=(c == 7))
                P.pub('pe', ins, ('p1k', g))
                if g >= 2:
                    P.wait(e, 'act', ('a1ek3', g - 2))
                elif g == 1:
                    P.wait(e, 'dve', ('v1v1', 0))
                if g == 0:
                    P.wait(e, 'dmawv', ('wv', 7))
                ins = None
                for j in range(2):
                    for c in range(8):
                        ins = e.matmul(W2[:, 512 * j:512 * (j + 1)],
                                       xlnT[:, 128 * c:128 * (c + 1)],
                                       WB[:, D * c + 512 * j:D * c + 512 * (j + 1)],
                                       start=(c == 0), stop=(c == 7))
                P.pub('pe', ins, ('p1v', g))
                P.wait(e, 'dve', ('v1ke', g))
                ins = None
                for hh in range(8):
                    ins = e.transpose(out=p0bf[0:64, 128 * hh:128 * (hh + 1)],
                                      in_=kqbf[:, 64 * hh:64 * (hh + 1)], identity=idbf[:])
                for hh in range(8):
                    ins = e.transpose(out=p0bf[64:128, 128 * hh:128 * (hh + 1)],
                                      in_=kqbf[:, 64 * (hh + 8):64 * (hh + 9)], identity=idbf[:])
                P.pub('pe', ins, ('p1kT', g))
            if g >= 1:
                h = g - 1
                parh = h % 2
                sqh, tth = seqof(h), ttof(h)
                P.wait(e, 'dve', 'pjscale')
                for b in range(4):
                    P.wait(e, 'act', ('v1kTe', h))
                    win = (W1 if b % 2 == 0 else W2)
                    if b == 0:
                        if g < NT:
                            P.wait(e, 'dve', ('v1ke', g))
                        else:
                            P.wait(e, 'act', ('a1ek2', g - 2))
                            P.wait(e, 'dve', ('v1sx', g - 2))
                    elif b == 1:
                        if g < NT:
                            P.wait(e, 'dve', ('v1v1', g))
                        else:
                            P.wait(e, 'act', ('a1ek3', g - 2))
                    else:
                        P.wait(e, 'act', ('a1ek%d' % (b - 2), h))
                    ins = None
                    for hd in (2 * b, 2 * b + 8, 2 * b + 1, 2 * b + 9):
                        r0_, r1_ = kqrows(hd)
                        col = expcol(hd) - 1024 * b
                        ins = e.matmul(win[:, col:col + 256],
                                       kqT2[parh][r0_:r1_, kqcol(hd):kqcol(hd) + 128],
                                       projTb[r0_:r1_, :], start=True, stop=True,
                                       tile_position=(r0_, 0))
                    P.pub('pe', ins, ('p1dd%d' % b, h))
                for half, nm in ((0, 'A'), (1, 'B')):
                    P.wait(e, 'act', ('a1ek3', h))
                    if half == 0 and h >= 1:
                        P.wait(e, 'dve', ('v1c0b', h - 1))
                    if half == 1:
                        P.wait(e, 'dve', ('v1c0a', h))
                    ins = None
                    for hh in range(8):
                        hd = 8 * half + hh
                        for mc in range(2):
                            s = mc * 8 + hh
                            ins = e.matmul(pall[:, c0_slot_psum(s):c0_slot_psum(s) + 65],
                                           expbuf[:, expcol(hd) + 128 * mc:expcol(hd) + 128 * mc + 128],
                                           v1buf2[parh][:, 65 * hd:65 * hd + 65],
                                           start=True, stop=True)
                    P.pub('pe', ins, ('p1c0%s' % nm, h))
                P.wait(e, 'act', ('a1ek2', h))
                ins = None
                for j in range(2):
                    ins = e.matmul(pall[0:1, 512 * j:512 * (j + 1)], onescol[:],
                                   xlnbf[h % 3][:, 512 * j:512 * (j + 1)], start=True, stop=True)
                P.pub('pe', ins, ('p1sx', h))
        P.wait(e, 'dve', ('v1gm3', NT - 1))
        P.wait(e, 'dve', ('v1sx', NT - 1))
        ins = e.transpose(out=pall[0:1, 0:128], in_=gmax128[:], identity=idf32[:])
        P.pub('pe', ins, 'gmaxT')
        P.wait(e, 'dmamsc', 'msc')
        ins = e.matmul(pall[:, 512:513], onesrow[:], Msc[0:1, 0:1], start=True, stop=True)
        P.pub('pe', ins, 'mbc')
        for s in range(nseq):
            P.wait(e, 'dve', 'gmaxfin')
            P.wait(e, 'dve', 'aeM')
            if s > 0:
                P.wait(e, 'dve', ('vsumv', s - 1))
            ins = None
            for c in range(8):
                ins = e.transpose(out=pall[:, c:c + 1],
                                  in_=sxl[0:1, s * D + 128 * c:s * D + 128 * (c + 1)],
                                  identity=onesrow[0:1, 0:1])
            P.pub('pe', ins, ('sxlT', s))
            P.wait(e, 'dve', ('vsxlT', s))
            ins = None
            for j in range(2):
                for c in range(8):
                    ins = e.matmul(pall[0:1, 1024 + 512 * j:1024 + 512 * (j + 1)],
                                   sxlT[:, c:c + 1],
                                   WB[:, D * c + 512 * j:D * c + 512 * (j + 1)],
                                   start=(c == 0), stop=(c == 7))
            P.pub('pe', ins, ('sumv', s))
        for g in range(NT + 1):
            par = g % 2
            if g < NT:
                sq, tt = seqof(g), ttof(g)
                if tt == 0:
                    for grp in range(2):
                        P.wait(e, 'dve', ('vsumv', nseq - 1))
                        P.wait(e, 'dve', 'aeM')
                        if grp == 1:
                            P.wait(e, 'dve', ('vctx0', sq))
                        elif g >= 1:
                            P.wait(e, 'dve', ('v2t1', g - 1))
                            P.wait(e, 'dve', ('v2x1', g - 1))
                            if g >= 2:
                                P.wait(e, 'dve', ('v2out', g - 2))
                        ins = None
                        for hh in range(8):
                            hd = 8 * grp + hh
                            for mc in range(2):
                                s = mc * 8 + hh
                                col = (s // 7) * 512 + (s % 7) * 65
                                ins = e.matmul(pall[:, col:col + 64], onesrow[:],
                                               rows4[0:1, sq * D + DH * hd:sq * D + DH * (hd + 1)],
                                               start=True, stop=True)
                        P.pub('pe', ins, ('ub', sq, grp))
                    P.wait(e, 'dve', ('vctx1', sq))
                    P.wait(e, 'dve', ('vctx', sq))
                    ins = None
                    for (d0, s0, n) in ((0, 0, 512), (512, 512, 512), (1024, 1024, 16),
                                        (1040, 0, 496), (1536, 496, 512), (2048, 1008, 32)):
                        srccol = (1040 + s0) if d0 >= 1040 else s0
                        ins = e.matmul(pall[0:1, d0:d0 + n], onescol[:],
                                       ctxaug[:, srccol:srccol + n], start=True, stop=True)
                    P.pub('pe', ins, ('scol', sq))
                    P.wait(e, 'dve', ('vscol', sq))
                    ins = None
                    bse = nseq * D
                    for (bk, n) in ((0, 455), (1, 455), (2, 130)):
                        ins = e.matmul(pall[:, 512 * bk:512 * bk + n], onesrow[:],
                                       rows4[0:1, bse + 455 * bk:bse + 455 * bk + n],
                                       start=True, stop=True)
                    P.pub('pe', ins, ('bscolbc', sq))
                P.wait(e, 'act', ('a2xln', g))
                if g >= 1:
                    P.wait(e, 'dve', ('v2attnTe', g - 1))
                ins = None
                for c in range(8):
                    ins = e.transpose(out=p0bf[:, 128 * c:128 * (c + 1)],
                                      in_=xlnbf[g % 3][:, 128 * c:128 * (c + 1)], identity=idbf[:])
                P.pub('pe', ins, ('p2xlnT', g))
                P.wait(e, 'act', ('v2xlnTe', g))
                P.wait(e, 'act', ('abscol', sq))
                if g >= 1:
                    P.wait(e, 'dve', ('v2x1', g - 1))
                if g == 0:
                    P.wait(e, 'dmawq', ('wq', 7))
                ins = None
                for j in range(2):
                    for c in range(8):
                        ins = e.matmul(W1[:, 512 * j:512 * (j + 1)],
                                       xlnT[:, 128 * c:128 * (c + 1)],
                                       WA[:, D * c + 512 * j:D * c + 512 * (j + 1)],
                                       start=(c == 0), stop=(c == 7))
                P.pub('pe', ins, ('p2q', g))
                P.wait(e, 'act', ('v2qe', g))
                ins = None
                for hh in range(8):
                    ins = e.transpose(out=p0bf[0:64, 128 * hh:128 * (hh + 1)],
                                      in_=kqbf[:, 64 * hh:64 * (hh + 1)], identity=idbf[:])
                for hh in range(8):
                    ins = e.transpose(out=p0bf[64:128, 128 * hh:128 * (hh + 1)],
                                      in_=kqbf[:, 64 * (hh + 8):64 * (hh + 9)], identity=idbf[:])
                P.pub('pe', ins, ('p2qT', g))
                for b in range(4):
                    P.wait(e, 'act', ('v2qTe', g))
                    win = (W1 if b % 2 == 0 else W2)
                    if b == 0:
                        P.wait(e, 'act', ('v2qe', g))
                        if tt == 0:
                            P.wait(e, 'act', ('abscol', sq))
                    elif b == 1:
                        if tt == 0:
                            P.wait(e, 'dve', ('vscol', sq))
                            P.wait(e, 'act', ('abscol', sq))
                        if g >= 2:
                            P.wait(e, 'dve', ('v2out', g - 2))
                    else:
                        P.wait(e, 'dve', ('v2rm%d' % (b - 2), g))
                    ins = None
                    for hd in (2 * b, 2 * b + 8, 2 * b + 1, 2 * b + 9):
                        r0_, r1_ = kqrows(hd)
                        col = expcol(hd) - 1024 * b
                        ins = e.matmul(win[:, col:col + 256],
                                       kqT2[par][r0_:r1_, kqcol(hd):kqcol(hd) + 128],
                                       projTb[r0_:r1_, :], start=True, stop=True,
                                       tile_position=(r0_, 0))
                    P.pub('pe', ins, ('p2ddtm%d' % b, g))
                for b in range(8):
                    P.wait(e, 'act', ('v2qTe', g))
                    if b == 0:
                        if tt == 0:
                            P.wait(e, 'dve', ('vscol', sq))
                        if g >= 1:
                            P.wait(e, 'dve', ('v2t1', g - 1))
                    if b < 3:
                        if g >= 1:
                            P.wait(e, 'act', ('a2g%d' % {0: 6, 1: 7, 2: 5}[b], g - 1))
                    else:
                        P.wait(e, 'act', ('a2g%d' % (b - 3), g))
                    win = pall[:, 2048 + 512 * (b % 3):2048 + 512 * (b % 3) + 512]
                    ins = None
                    for j in range(2):
                        hd = 2 * b + j
                        r0_, r1_ = kqrows(hd)
                        for mc in range(2):
                            ins = e.matmul(win[:, 256 * j + 128 * mc:256 * j + 128 * mc + 128],
                                           projTb[r0_:r1_, 128 * mc:128 * (mc + 1)],
                                           kqT2[par][r0_:r1_, kqcol(hd):kqcol(hd) + 128],
                                           start=True, stop=True, tile_position=(r0_, 0))
                    P.pub('pe', ins, ('p2ddmm%d' % b, g))
                P.wait(e, 'dve', ('vctx', sq))
                for (s0, ns) in GC_PIECES:
                    P.wait(e, 'act', ('a2g%d' % (6 if s0 == 0 else 7), g))
                    ins = None
                    for hd in range(s0, s0 + ns):
                        grp = hd // 8
                        gcb = 512 * (hd // 2) + 256 * (hd % 2)
                        for mc in range(2):
                            s = mc * 8 + (hd % 8)
                            ins = e.matmul(pall[:, gc_col(hd):gc_col(hd) + 65],
                                           expbuf[:, gcb + 128 * mc:gcb + 128 * mc + 128],
                                           ctxaug[:, grp * 1040 + s * 65:grp * 1040 + s * 65 + 65],
                                           start=(mc == 0), stop=(mc == 1))
                    P.pub('pe', ins, ('p2gc%d' % s0, g))
            if g < NT:
                if g == 0:
                    P.wait(e, 'dmawo', ('wo', 7))
                    P.wait(e, 'dmabo', 'bo')
                P.wait(e, 'dve', ('v2rm2', g))
                for j in range(2):
                    e.matmul(W1[:, 512 * j:512 * (j + 1)], onesrow[:],
                             bo_row[0:1, 512 * j:512 * (j + 1)], start=True, stop=False)
                P.wait(e, 'dve', ('v2attnA', g))
                ins = None
                for c in range(4):
                    ins = e.transpose(out=p0bf[:, 128 * c:128 * (c + 1)],
                                      in_=attnbf[:, 128 * c:128 * (c + 1)], identity=idbf[:])
                P.wait(e, 'dve', ('v2attn', g))
                for c in range(4, 8):
                    ins = e.transpose(out=p0bf[:, 128 * c:128 * (c + 1)],
                                      in_=attnbf[:, 128 * c:128 * (c + 1)], identity=idbf[:])
                P.pub('pe', ins, ('p2attnT', g))
                P.wait(e, 'dve', ('v2attnTe', g))
                ins = None
                for j in range(2):
                    for c in range(8):
                        ins = e.matmul(W1[:, 512 * j:512 * (j + 1)],
                                       xlnT[:, 128 * c:128 * (c + 1)],
                                       WB[:, D * c + 512 * j:D * c + 512 * (j + 1)],
                                       start=False, stop=(c == 7))
                P.pub('pe', ins, ('p2y', g))
            if g >= 1:
                h = g - 1
                P.wait(e, 'dve', ('v2aq', h))
                if g == NT:
                    P.wait(e, 'dve', ('v2attnTe', h))
                ins = None
                for c in range(8):
                    ins = e.transpose(out=p0bf[:, 128 * c:128 * (c + 1)],
                                      in_=aqbf[:, 128 * c:128 * (c + 1)], identity=idbf[:])
                P.pub('pe', ins, ('p2aqT', h))
                P.wait(e, 'dve', ('v2aqTe', h))
                if g < NT:
                    P.wait(e, 'dve', ('v2rm3', g))
                elif h >= 1:
                    P.wait(e, 'dve', ('v2out', h - 1))
                ins = None
                for j in range(2):
                    for c in range(8):
                        ins = e.matmul(W2[:, 512 * j:512 * (j + 1)],
                                       xlnT[:, 128 * c:128 * (c + 1)],
                                       WC[:, D * c + 512 * j:D * c + 512 * (j + 1)],
                                       start=(c == 0), stop=(c == 7))
                P.pub('pe', ins, ('p2y2', h))

    streams = [stream_sync, stream_gp, stream_dve, stream_act, stream_pe]

    sch.begin(True)
    fake = _FakeEng()
    for fn in streams:
        fn(fake)

    sch.begin(False)
    with nc.Block() as block:
        @block.sync
        def _(eng):
            stream_sync(eng)

        @block.gpsimd
        def _(eng):
            stream_gp(eng)

        @block.vector
        def _(eng):
            stream_dve(_DrainEng(eng))

        @block.scalar
        def _(eng):
            stream_act(_DrainEng(eng))

        @block.tensor
        def _(eng):
            stream_pe(eng)

    return nc


def _run(x, wq, wk, wv, wo, bo, proj, wffn, nseq, ntiles, cores):
    S = ntiles * T
    rows = nseq * S
    ncores = len(cores)
    xs = np.ascontiguousarray(np.asarray(x, np.float32)).reshape(ncores * nseq, S, D)
    wfT = np.ascontiguousarray(np.asarray(wffn, np.float32).T)
    pjT = np.ascontiguousarray(np.asarray(proj, np.float32).T)
    bo2 = np.ascontiguousarray(np.asarray(bo, np.float32).reshape(1, D))
    wq = np.ascontiguousarray(np.asarray(wq, np.float32))
    wk = np.ascontiguousarray(np.asarray(wk, np.float32))
    wv = np.ascontiguousarray(np.asarray(wv, np.float32))
    wo = np.ascontiguousarray(np.asarray(wo, np.float32))
    nc = build_program(nseq, ntiles)
    in_maps = []
    for c in range(ncores):
        shard = np.ascontiguousarray(xs[c * nseq:(c + 1) * nseq].reshape(rows, D))
        in_maps.append({"x": shard, "wq": wq, "wk": wk, "wv": wv, "wo": wo,
                        "bo": bo2, "projT": pjT, "wfT": wfT})
    res = run_bass_kernel_spmd(nc, in_maps, core_ids=cores)
    outs = [res.results[c]["out"].reshape(nseq, S, D) for c in range(ncores)]
    return np.concatenate(outs, axis=0)


def kernel(L0_attn_x, L0_attn_wq, L0_attn_wk, L0_attn_wv, L0_attn_wo,
           L0_attn_bo, L0_attn_proj, L0_ffn_weight):
    out = _run(L0_attn_x, L0_attn_wq, L0_attn_wk, L0_attn_wv, L0_attn_wo,
               L0_attn_bo, L0_attn_proj, L0_ffn_weight,
               nseq=4, ntiles=16, cores=list(range(8)))
    return out.reshape(32, 2048, D)


# version 21 (final)
# speedup vs baseline: 1.0337x; 1.0337x over previous
import numpy as np
import ml_dtypes
import concourse.bass as bass
import concourse.mybir as mybir
from concourse.bass_utils import run_bass_kernel_spmd

F32 = mybir.dt.float32
BF16 = mybir.dt.bfloat16
I32 = mybir.dt.int32
AF = mybir.ActivationFunctionType
ALU = mybir.AluOpType
AX = mybir.AxisListType

D = 1024
H = 16
DH = 64
MF = 256
RATIO = 0.0625
KEPS = 1e-4
C2H = 0.0625
LNEPS = 1e-5
B_EPS = RATIO * KEPS
T = 128
NCORES = 8


class _FakeInstr:
    def then_inc(self, *a, **k):
        return self


class _FakeEng:
    def __getattr__(self, name):
        def f(*a, **k):
            return _FakeInstr()
        return f


_NO_DRAIN = {"wait_ge", "drain", "nop", "memset", "dma_start", "then_inc",
             "collective_compute", "affine_select"}


class _DrainEng:

    def __init__(self, e):
        self._e = e

    def __getattr__(self, name):
        real = getattr(self._e, name)
        if name.startswith("_") or name in _NO_DRAIN or not callable(real):
            return real

        def f(*a, **k):
            self._e.drain()
            return real(*a, **k)
        return f


class Sched:

    def __init__(self, nc):
        self.nc = nc
        self.pass0 = True
        self.sems = {}
        self.counts = {}
        self.marks = {}

    def sem(self, name):
        if name not in self.sems:
            self.sems[name] = self.nc.alloc_semaphore(name)
        return self.sems[name]

    def begin(self, pass0):
        self.pass0 = pass0
        self.counts = {}

    def pub(self, name, instr, key=None, step=1):
        self.sem(name)
        n = self.counts.get(name, 0) + step
        self.counts[name] = n
        if key is not None:
            k = (name,) + (key if isinstance(key, tuple) else (key,))
            if self.pass0:
                self.marks[k] = n
            else:
                assert self.marks.get(k) == n, f"mark drift {k}: {self.marks.get(k)} vs {n}"
        if (not self.pass0):
            assert instr is not None and not isinstance(instr, _FakeInstr), f"null instr pub {name} {key}"
        if (not self.pass0) and instr is not None:
            instr.then_inc(self.sems[name], step)

    def wait(self, eng, name, key):
        if self.pass0:
            return
        k = (name,) + (key if isinstance(key, tuple) else (key,))
        v = self.marks.get(k)
        assert v is not None, f"missing mark {k}"
        eng.wait_ge(self.sems[name], v)


def build_program(nseq, ntiles):
    NT = nseq * ntiles
    S = ntiles * T
    ROWS = nseq * S

    nc = bass.Bass("TRN2", target_bir_lowering=False)

    x_hbm = nc.dram_tensor("x", [ROWS, D], F32, kind="ExternalInput").ap()
    wq_hbm = nc.dram_tensor("wq", [D, D], F32, kind="ExternalInput").ap()
    wk_hbm = nc.dram_tensor("wk", [D, D], F32, kind="ExternalInput").ap()
    wv_hbm = nc.dram_tensor("wv", [D, D], F32, kind="ExternalInput").ap()
    wo_hbm = nc.dram_tensor("wo", [D, D], F32, kind="ExternalInput").ap()
    bo_hbm = nc.dram_tensor("bo", [1, D], F32, kind="ExternalInput").ap()
    pjT_hbm = nc.dram_tensor("projT", [DH, MF], F32, kind="ExternalInput").ap()
    wfT_hbm = nc.dram_tensor("wfT", [D, D], F32, kind="ExternalInput").ap()
    out_hbm = nc.dram_tensor("out", [ROWS, D], F32, kind="ExternalOutput").ap()
    ccin = nc.dram_tensor("ccin", [1, 32], F32)
    ccout = nc.dram_tensor("ccout", [1, 32], F32)

    p0bf = nc.alloc_psum_tensor("p0bf", [128, 1024], BF16)
    pall = nc.alloc_psum_tensor("pall", [128, 3584], F32)
    assert nc.psum_banks_remaining == 0, nc.psum_banks_remaining
    W1 = pall[:, 0:1024]
    W2 = pall[:, 1024:2048]
    C0P = pall[:, 2048:3584]
    p1bf = pall[:, 3072:3584].bitcast(BF16)

    A = nc.alloc_sbuf_tensor
    idbf = A("idbf", [128, 128], BF16)
    idf32 = A("idf32", [128, 128], F32)
    onescol = A("onescol", [128, 1], BF16)
    onesrow = A("onesrow", [1, 128], F32)
    WA = A("WA", [128, 8 * D], BF16)
    WB = A("WB", [128, 8 * D], BF16)
    WC = A("WC", [128, 8 * D], BF16)
    projTb = A("projTb", [128, MF], BF16)
    bo_row = A("bo_row", [1, D], F32)
    wsbc = A("wsbc", [128, 1], F32)
    wsd127 = A("wsd127", [128, 1], F32)
    iws = A("iws", [128, 1], F32)
    absacc = A("absacc", [128, 8], F32)
    abssum = A("abssum", [128, 1], F32)
    wsrow = A("wsrow", [1, 1], F32)

    xb = [A("xb0", [128, D], F32), A("xb1", [128, D], F32), A("xb2", [128, D], F32)]
    bnst = [A("bns0", [128, 12], F32), A("bns1", [128, 12], F32)]
    mv = [A("mv0", [128, 2], F32), A("mv1", [128, 2], F32)]
    rstd = [A("rstd0", [128, 1], F32), A("rstd1", [128, 1], F32)]
    bbv = [A("bb0", [128, 1], F32), A("bb1", [128, 1], F32)]
    bnst2 = [A("bns20", [128, 12], F32), A("bns21", [128, 12], F32)]
    mv2 = [A("mv20", [128, 2], F32), A("mv21", [128, 2], F32)]
    rstd2 = [A("rstd20", [128, 1], F32), A("rstd21", [128, 1], F32)]
    bbv2 = [A("bb20", [128, 1], F32), A("bb21", [128, 1], F32)]
    xlnbf = [A("xlnbf0", [128, D], BF16), A("xlnbf1", [128, D], BF16),
             A("xlnbf2", [128, D], BF16)]
    xlnT = A("xlnT", [128, D], BF16)
    xlnT2 = A("xlnT2", [128, D], BF16)
    kqbf = A("kqbf", [128, D], BF16)
    kqT2 = [A("kqT0", [128, D], BF16), A("kqT1", [128, D], BF16)]
    scrK = A("scrK", [128, D], F32)
    sqjunk = A("sqjunk", [128, D], F32)
    diag = A("diag", [128, H], F32)
    akbuf = A("akbuf", [128, H], F32)
    expbuf = A("expbuf", [128, 4 * D], BF16)
    v1buf2 = [A("v1buf0", [128, H * 65], BF16), A("v1buf1", [128, H * 65], BF16)]
    gmax128 = A("gmax128", [128, 1], F32)
    redb = A("redb", [128, 1], F32)
    cx0 = A("cx0", [128, nseq * 2080], F32)
    sxl = A("sxl", [1, nseq * D], F32)
    Mloc = A("Mloc", [1, 1], F32)
    mrow = A("mrow", [1, 32], F32)
    Msc = A("Msc", [1, 32], F32)
    eM128 = A("eM128", [128, 1], F32)
    sxlT = A("sxlT", [128, 8], BF16)
    rows4 = A("rows4", [1, nseq * D + 1408], F32)
    ctxaug = A("ctxaug", [128, 2080], BF16)
    bscol = A("bscol", [128, 1365], F32)
    t1b = A("t1b", [128, H * 65], F32)
    rowmax = A("rowmax", [128, H], F32)
    mvsave = A("mvsave", [128, 2 * ntiles * nseq], F32)
    drb = A("drb", [128, H], F32)
    aexp = A("aexp", [128, H], F32)
    anb = A("anb", [128, H], F32)
    dinv = A("dinv", [128, H], F32)
    attnbf = A("attnbf", [128, D], BF16)
    xln2f = A("xln2f", [128, D], F32)
    amaxb = A("amaxb", [128, 1], F32)
    am8 = A("am8", [128, 1], F32)
    rzb = A("rzb", [128, 1], F32)
    s2b = A("s2b", [128, 1], F32)
    aqi = A("aqi", [128, D], I32)
    aqbf = A("aqbf", [128, D], BF16)
    tmp1 = A("tmp1", [128, 1], F32)
    nmr = A("nmr", [128, 1], F32)
    lnepsb = A("lnepsb", [128, 1], F32)

    sch = Sched(nc)
    P = sch

    def seqof(g):
        return g // ntiles

    def ttof(g):
        return g % ntiles

    C0_PIECES = [(0, 7), (7, 7), (14, 2)]

    def c0_slot_psum(s):
        return 2048 + (s // 7) * 512 + (s % 7) * 65

    def cx0_seq(s):
        return cx0[:, s * 2080:(s + 1) * 2080]

    GC_PIECES = [(0, 7), (7, 7), (14, 2)]

    def expcol(h):
        return 1024 * ((h % 8) // 2) + 256 * (h % 2) + 512 * (h // 8)

    def kqcol(h):
        return 128 * (h % 8)

    def kqrows(h):
        return (0, 64) if h < 8 else (64, 128)

    def gc_col(h):
        return 2048 + (h // 7) * 512 + (h % 7) * 65

    def bs_col(h):
        return (h // 7) * 455 + (h % 7) * 65


    def stream_sync(e):
        for c in range(8):
            if c >= 2:
                P.wait(e, 'dve', ('ffn1', c - 2))
            ins = e.dma_start(out=xb[c % 2][:], in_=wfT_hbm[128 * c:128 * (c + 1), :])
            P.pub('dmax%d' % (c % 2), ins, ('wf1', c), 16)
        for c in range(8):
            if c < 2:
                P.wait(e, 'dve', ('ffn1', 6 + c))
            else:
                P.wait(e, 'dve', ('ffn2', c - 2))
            ins = e.dma_start(out=xb[c % 2][:], in_=wfT_hbm[128 * c:128 * (c + 1), :])
            P.pub('dmax%d' % (c % 2), ins, ('wf2', c), 16)
        ins = e.dma_start(out=bo_row[:], in_=bo_hbm[:, :])
        P.pub('dmabo', ins, 'bo', 16)
        for g in range(NT):
            if g >= 3:
                P.wait(e, 'act', ('a1xln', g - 3))
                P.wait(e, 'dve', ('v1bn', g - 3))
            elif g < 2:
                P.wait(e, 'dve', ('ffn2', 6 + g))
            r0 = seqof(g) * S + ttof(g) * T
            ins = e.dma_start(out=xb[g % 3][:], in_=x_hbm[r0:r0 + T, :])
            P.pub('dmax%d' % (g % 3), ins, ('x1', g), 16)
        for g in range(NT):
            if g >= 3:
                P.wait(e, 'dmao%d' % ((g - 3) % 3), ('o', g - 3))
            else:
                P.wait(e, 'act', ('a1xln', NT - 1))
                P.wait(e, 'dve', ('v1bn', NT - 1))
                P.wait(e, 'dve', ('v1sx', NT - 1))
            r0 = seqof(g) * S + ttof(g) * T
            ins = e.dma_start(out=xb[g % 3][:], in_=x_hbm[r0:r0 + T, :])
            P.pub('dmax%d' % (g % 3), ins, ('x2', g), 16)

    def stream_gp(e):
        e.memset(idbf[:], 0.0).then_inc(P.sem('gp'), 1)
        e.memset(idf32[:], 0.0).then_inc(P.sem('gp'), 1)
        e.memset(onescol[:], 1.0).then_inc(P.sem('gp'), 1)
        e.memset(lnepsb[:], LNEPS).then_inc(P.sem('gp'), 1)
        e.memset(onesrow[:], 1.0).then_inc(P.sem('gp'), 1)
        P.counts['gp'] = P.counts.get('gp', 0) + 5
        if not P.pass0:
            e.wait_ge(P.sem('gp'), 5)
        e.affine_select(out=idbf[:], in_=idbf[:], compare_op=ALU.not_equal,
                        fill=1.0, base=0, pattern=[[-1, 128]], channel_multiplier=1)
        ins = e.affine_select(out=idf32[:], in_=idf32[:], compare_op=ALU.not_equal,
                              fill=1.0, base=0, pattern=[[-1, 128]], channel_multiplier=1)
        P.pub('gp', ins, 'ident')
        for c in range(8):
            ins = e.dma_start(out=WA[:, D * c:D * (c + 1)], in_=wk_hbm[128 * c:128 * (c + 1), :])
            P.pub('dmawk', ins, ('wk', c), 16)
        for c in range(8):
            ins = e.dma_start(out=WB[:, D * c:D * (c + 1)], in_=wv_hbm[128 * c:128 * (c + 1), :])
            P.pub('dmawv', ins, ('wv', c), 16)
        e.dma_start(out=projTb[0:64, :], in_=pjT_hbm[:, :]).then_inc(P.sem('dmapj'), 16)
        P.counts['dmapj'] = P.counts.get('dmapj', 0) + 16
        ins = e.dma_start(out=projTb[64:128, :], in_=pjT_hbm[:, :])
        P.pub('dmapj', ins, 'pj', 16)
        P.wait(e, 'dve', 'mrow')
        ins = e.dma_start(out=ccin.ap(), in_=mrow[:])
        P.pub('dmacc', ins, 'ccin', 16)
        P.wait(e, 'dmacc', 'ccin')
        ins = e.collective_compute(
            "AllReduce", ALU.max,
            replica_groups=[list(range(NCORES))],
            ins=[ccin.ap().opt()],
            outs=[ccout.ap().opt()],
        )
        P.pub('cc', ins, 'ar', 1)
        P.wait(e, 'cc', 'ar')
        ins = e.dma_start(out=Msc[:], in_=ccout.ap())
        P.pub('dmamsc', ins, 'msc', 16)
        P.wait(e, 'pe', ('p1k', NT - 1))
        P.wait(e, 'pe', ('sumv', nseq - 1))
        for c in range(8):
            ins = e.dma_start(out=WA[:, D * c:D * (c + 1)], in_=wq_hbm[128 * c:128 * (c + 1), :])
            P.pub('dmawq', ins, ('wq', c), 16)
        for c in range(8):
            ins = e.dma_start(out=WB[:, D * c:D * (c + 1)], in_=wo_hbm[128 * c:128 * (c + 1), :])
            P.pub('dmawo', ins, ('wo', c), 16)

    def ln_stats(e, g, tag, sqtag, bn, mvb, skip_sum=False):
        par = g % 2
        xbuf = xb[g % 3]
        if not skip_sum:
            e.tensor_reduce(out=bn[par][:, 2:3], in_=xbuf[:], axis=AX.X, op=ALU.add)
        P.wait(e, 'act', (sqtag, g))
        e.tensor_scalar(out=mvb[par][:, 0:1], in0=bn[par][:, 2:3], scalar1=-1.0 / D,
                        scalar2=None, op0=ALU.mult)
        e.tensor_scalar(out=bn[par][:, 1:2], in0=mvb[par][:, 0:1],
                        scalar1=mvb[par][:, 0:1], scalar2=None, op0=ALU.mult)
        ins = e.tensor_scalar(out=mvb[par][:, 1:2], in0=bn[par][:, 0:1],
                              scalar1=1.0 / D, scalar2=bn[par][:, 1:2],
                              op0=ALU.mult, op1=ALU.subtract)
        P.pub('dve', ins, (tag, g))

    def ln_apply(e, g, tag, bntag, mvb, rs, outbuf, outslice=None):
        par = g % 2
        P.wait(e, 'dve', (bntag, g))
        e.activation(out=tmp1[:], in_=mvb[par][:, 1:2], func=AF.Ln, bias=lnepsb[:])
        e.activation(out=rs[par][:], in_=tmp1[:], func=AF.Exp, scale=-0.5)
        e.activation(out=nmr[:], in_=mvb[par][:, 0:1], func=AF.Identity,
                     scale=rs[par][:])
        ins = e.activation(out=outbuf[:] if outslice is None else outslice,
                           in_=xb[g % 3][:], func=AF.Identity, scale=rs[par][:],
                           bias=nmr[:])
        return ins

    def stream_dve(e):
        for c in range(8):
            P.wait(e, 'dmax%d' % (c % 2), ('wf1', c))
            ins = e.tensor_reduce(out=absacc[:, c:c + 1], in_=xb[c % 2][:],
                                  axis=AX.X, op=ALU.add, apply_absolute_value=True)
            P.pub('dve', ins, ('ffn1', c))
        ins = e.tensor_reduce(out=abssum[:], in_=absacc[:], axis=AX.X, op=ALU.add)
        P.pub('dve', ins, 'abssum')
        P.wait(e, 'pe', 'wsT')
        e.tensor_reduce(out=wsrow[:], in_=pall[0:1, 0:128], axis=AX.X, op=ALU.add)
        ins = e.tensor_scalar(out=wsrow[:], in0=wsrow[:], scalar1=1.0 / (D * D),
                              scalar2=1e-8, op0=ALU.mult, op1=ALU.add)
        P.pub('dve', ins, 'wsrow')
        P.wait(e, 'pe', 'wsbc')
        e.tensor_copy(wsbc[:], pall[:, 512:513])
        e.tensor_scalar(out=wsd127[:], in0=wsbc[:], scalar1=1.0 / 127.0,
                        scalar2=None, op0=ALU.mult)
        ins = e.reciprocal(out=iws[:], in_=wsbc[:])
        P.pub('dve', ins, 'iws')
        P.wait(e, 'dmapj', 'pj')
        ins = e.tensor_scalar(out=projTb[:], in0=projTb[:], scalar1=DH ** -0.25,
                              scalar2=None, op0=ALU.mult)
        P.pub('dve', ins, 'pjscale')
        for c in range(8):
            P.wait(e, 'dmax%d' % (c % 2), ('wf2', c))
            e.tensor_scalar(out=scrK[:], in0=xb[c % 2][:], scalar1=iws[:],
                            scalar2=None, op0=ALU.mult)
            e.tensor_scalar(out=scrK[:], in0=scrK[:], scalar1=12582912.0,
                            scalar2=-12582912.0, op0=ALU.add, op1=ALU.add)
            e.tensor_copy(aqi[:], scrK[:])
            ins = e.tensor_scalar(out=WC[:, D * c:D * (c + 1)], in0=aqi[:],
                                  scalar1=1.0, scalar2=-1.0, op0=ALU.min, op1=ALU.max)
            P.pub('dve', ins, ('ffn2', c))
        for g in range(-1, NT + 1):
            if 0 <= g + 1 < NT:
                gn = g + 1
                P.wait(e, 'dmax%d' % (gn % 3), ('x1', gn))
                if gn >= 2:
                    P.wait(e, 'act', ('a1xln', gn - 2))
                ln_stats(e, gn, 'v1bn', 'a1sq', bnst, mv)
                ins = e.tensor_copy(mvsave[:, 2 * gn:2 * gn + 2], mv[gn % 2][:, 0:2])
                P.pub('dve', ins, ('v1mvs', gn))
            if 0 <= g < NT:
                par = g % 2
                P.wait(e, 'act', ('a1ksq', g))
                ins = e.tensor_copy(kqbf[:], W1[:])
                P.pub('dve', ins, ('v1ke', g))
                ins = e.tensor_reduce(out=diag[:], in_=scrK[:].rearrange("p (h d) -> p h d", h=H),
                                      axis=AX.X, op=ALU.add)
                P.pub('dve', ins, ('v1diag', g))
                P.wait(e, 'pe', ('p1v', g))
                P.wait(e, 'act', ('a1ak', g))
                v3 = W2[:].rearrange("p (h d) -> p h d", h=H)
                akb = akbuf[:].unsqueeze(2).broadcast_to([128, H, DH])
                dst = v1buf2[par][:].rearrange("p (h c) -> p h c", h=H)[:, :, 0:DH]
                e.tensor_tensor(out=dst, in0=v3, in1=akb, op=ALU.mult)
                ins = e.tensor_copy(v1buf2[par][:].rearrange("p (h c) -> p h c", h=H)[:, :, 64:65],
                                    akbuf[:].unsqueeze(2))
                P.pub('dve', ins, ('v1v1', g))
            if 1 <= g <= NT:
                h = g - 1
                sqh, tth = seqof(h), ttof(h)
                for b in range(3):
                    P.wait(e, 'act', ('a1ek%d' % b, h))
                    e.tensor_reduce(out=redb[:], in_=expbuf[:, 1024 * b:1024 * (b + 1)],
                                    axis=AX.X, op=ALU.max)
                    if h == 0 and b == 0:
                        ins = e.tensor_copy(gmax128[:], redb[:])
                    else:
                        ins = e.tensor_tensor(out=gmax128[:], in0=gmax128[:], in1=redb[:],
                                              op=ALU.max)
                    P.pub('dve', ins, ('v1gm%d' % b, h))
                for half, nm in ((0, 'a'), (1, 'b')):
                    P.wait(e, 'pe', ('p1c0%s' % nm.upper(), h))
                    base = sqh * 2080 + half * 1040
                    ins = None
                    for (s0, ns) in C0_PIECES:
                        srcp = pall[:, c0_slot_psum(s0):c0_slot_psum(s0) + ns * 65]
                        dstp = cx0[:, base + s0 * 65:base + (s0 + ns) * 65]
                        if tth == 0:
                            ins = e.tensor_copy(dstp, srcp)
                        else:
                            ins = e.tensor_tensor(out=dstp, in0=dstp, in1=srcp, op=ALU.add)
                    P.pub('dve', ins, ('v1c0%s' % nm, h))
                P.wait(e, 'pe', ('p1sx', h))
                srcp = pall[0:1, 0:1024]
                dstp = sxl[0:1, sqh * D:(sqh + 1) * D]
                if tth == 0:
                    ins = e.tensor_copy(dstp, srcp)
                else:
                    ins = e.tensor_tensor(out=dstp, in0=dstp, in1=srcp, op=ALU.add)
                P.pub('dve', ins, ('v1sx', h))
                P.wait(e, 'act', ('a1ek3', h))
                e.tensor_reduce(out=redb[:], in_=expbuf[:, 3072:4096],
                                axis=AX.X, op=ALU.max)
                ins = e.tensor_tensor(out=gmax128[:], in0=gmax128[:], in1=redb[:],
                                      op=ALU.max)
                P.pub('dve', ins, ('v1gm3', h))
        P.wait(e, 'pe', 'gmaxT')
        ins = e.tensor_reduce(out=Mloc[:], in_=pall[0:1, 0:128], axis=AX.X, op=ALU.max)
        P.pub('dve', ins, 'gmaxfin')
        ins = e.tensor_copy(mrow[:], Mloc[:].broadcast_to([1, 32]))
        P.pub('dve', ins, 'mrow')
        P.wait(e, 'pe', 'mbc')
        ins = e.reciprocal(out=eM128[:], in_=pall[:, 512:513])
        P.pub('dve', ins, 'aeM')
        for s in range(nseq):
            P.wait(e, 'pe', ('sxlT', s))
            ins = e.tensor_copy(sxlT[:], pall[:, 0:8])
            P.pub('dve', ins, ('vsxlT', s))
            P.wait(e, 'pe', ('sumv', s))
            ins = e.tensor_copy(rows4[0:1, s * D:(s + 1) * D], pall[0:1, 1024:2048])
            P.pub('dve', ins, ('vsumv', s))
        for g in range(-1, NT + 2):
            if 1 <= g <= NT:
                h = g - 1
                parh = h % 2
                P.wait(e, 'pe', ('p2y', h))
                xh = xb[h % 3]
                ins = e.tensor_tensor(out=xh[:], in0=xh[:], in1=W1[:], op=ALU.add)
                P.pub('dve', ins, ('v2x1', h))
            if 2 <= g <= NT + 1:
                hm = g - 2
                P.wait(e, 'pe', ('p2y2', hm))
                xm = xb[hm % 3]
                ins = e.scalar_tensor_tensor(out=xm[:], in0=W2[:], scalar=s2b[:],
                                             in1=xm[:], op0=ALU.mult, op1=ALU.add)
                P.pub('dve', ins, ('v2out', hm))
            if 0 <= g < NT and ttof(g) == 0:
                sq = seqof(g)
                P.wait(e, 'dve', 'aeM')
                for grp in range(2):
                    P.wait(e, 'pe', ('ub', sq, grp))
                    last = None
                    for (s0, ns) in C0_PIECES:
                        ubp = pall[:, (s0 // 7) * 512 + (s0 % 7) * 65:
                                   (s0 // 7) * 512 + (s0 % 7) * 65 + ns * 65]
                        ubv = ubp.rearrange("p (s c) -> p s c", c=65)[:, :, 0:DH]
                        t1v = t1b[:, s0 * 65:(s0 + ns) * 65].rearrange("p (s c) -> p s c", c=65)[:, :, 0:DH]
                        e.tensor_scalar(out=t1v, in0=ubv, scalar1=KEPS, scalar2=None, op0=ALU.mult)
                        cb = sq * 2080 + grp * 1040
                        cxv = cx0[:, cb + s0 * 65:cb + (s0 + ns) * 65].rearrange("p (s c) -> p s c", c=65)[:, :, 0:DH]
                        ctv = ctxaug[:, grp * 1040 + s0 * 65:grp * 1040 + (s0 + ns) * 65].rearrange("p (s c) -> p s c", c=65)[:, :, 0:DH]
                        last = e.scalar_tensor_tensor(out=ctv, in0=cxv, scalar=eM128[:],
                                                      in1=t1v, op0=ALU.mult, op1=ALU.add)
                    P.pub('dve', last, ('vctx%d' % grp, sq))
                cx64 = cx0_seq(sq)[:].rearrange("p (s c) -> p s c", c=65)[:, :, 64:65]
                ct64 = ctxaug[:].rearrange("p (s c) -> p s c", c=65)[:, :, 64:65]
                ins = e.tensor_scalar(out=ct64, in0=cx64, scalar1=eM128[:],
                                      scalar2=KEPS * float(S), op0=ALU.mult, op1=ALU.add)
                P.pub('dve', ins, ('vctx', sq))
                P.wait(e, 'pe', ('scol', sq))
                bse = nseq * D
                ins = None
                for (d0, a0, b0, n) in ((0, 0, 520, 455), (455, 455, 975, 65),
                                        (520, 1040, 1560, 390), (910, 1430, 1950, 130)):
                    e.tensor_copy(rows4[0:1, bse + d0:bse + d0 + n], pall[0:1, a0:a0 + n])
                    ins = e.tensor_tensor(out=rows4[0:1, bse + d0:bse + d0 + n],
                                          in0=rows4[0:1, bse + d0:bse + d0 + n],
                                          in1=pall[0:1, b0:b0 + n], op=ALU.add)
                P.pub('dve', ins, ('vscol', sq))
            if 0 <= g < NT:
                par = g % 2
                P.wait(e, 'act', ('a2qsq', g))
                ins = e.tensor_reduce(out=diag[:], in_=scrK[:].rearrange("p (h d) -> p h d", h=H),
                                      axis=AX.X, op=ALU.add)
                P.pub('dve', ins, ('v2diagq', g))
            if 0 <= g < NT:
                par = g % 2
                for b in range(4):
                    P.wait(e, 'pe', ('p2ddtm%d' % b, g))
                    win = (W1 if b % 2 == 0 else W2)
                    ins = e.tensor_reduce(out=rowmax[:, 4 * b:4 * b + 4],
                                          in_=win[:].rearrange("p (h m) -> p h m", h=4),
                                          axis=AX.X, op=ALU.max)
                    P.pub('dve', ins, ('v2rm%d' % b, g))
                rmv = rowmax[:].rearrange("p (b q) -> p b q", q=4)
                e.scalar_tensor_tensor(out=drb[:, 0:8], in0=diag[:, 0:8], scalar=C2H,
                                       in1=rmv[:, :, 0:2], op0=ALU.mult, op1=ALU.add)
                ins = e.scalar_tensor_tensor(out=drb[:, 8:16], in0=diag[:, 8:16], scalar=C2H,
                                             in1=rmv[:, :, 2:4], op0=ALU.mult, op1=ALU.add)
                P.pub('dve', ins, ('v2dr', g))
            if 1 <= g <= NT:
                h = g - 1
                parh = h % 2
                if h >= 2:
                    P.wait(e, 'act', ('a2xln2', h - 2))
                ln_stats(e, h, 'v2lbn', 'a2lsq', bnst2, mv2, skip_sum=True)
            if 0 <= g < NT:
                P.wait(e, 'act', ('a2aexp', g))
                ins = e.tensor_scalar(out=anb[:], in0=aexp[:], scalar1=RATIO,
                                      scalar2=None, op0=ALU.mult)
                P.pub('dve', ins, ('v2an', g))
            if 0 <= g < NT:
                sq = seqof(g)
                P.wait(e, 'act', ('abscol', sq))
                for (s0, ns) in GC_PIECES:
                    P.wait(e, 'pe', ('p2gc%d' % s0, g))
                    gc3 = pall[:, gc_col(s0):gc_col(s0) + ns * 65].rearrange("p (s c) -> p s c", c=65)
                    an3 = anb[:, s0:s0 + ns].unsqueeze(2).broadcast_to([128, ns, 65])
                    t1v = t1b[:, s0 * 65:(s0 + ns) * 65].rearrange("p (s c) -> p s c", c=65)
                    e.tensor_tensor(out=t1v, in0=gc3, in1=an3, op=ALU.mult)
                    bsv = bscol[:, bs_col(s0):bs_col(s0) + ns * 65].rearrange("p (s c) -> p s c", c=65)
                    last = e.tensor_tensor(out=t1v, in0=t1v, in1=bsv, op=ALU.add)
                    if s0 == 7:
                        e.reciprocal(out=dinv[:, 0:8],
                                     in_=t1b[:, 0:8 * 65].rearrange("p (h c) -> p h c", c=65)[:, :, 64:65].rearrange("p h c -> p (h c)"))
                        t1n = t1b[:, 0:8 * 65].rearrange("p (h c) -> p h c", c=65)[:, :, 0:DH]
                        dv3 = dinv[:, 0:8].unsqueeze(2).broadcast_to([128, 8, DH])
                        ins = e.tensor_tensor(out=attnbf[:, 0:8 * DH].rearrange("p (h d) -> p h d", h=8),
                                              in0=t1n, in1=dv3, op=ALU.mult)
                        P.pub('dve', ins, ('v2attnA', g))
                P.pub('dve', last, ('v2t1', g))
                e.reciprocal(out=dinv[:, 8:16],
                             in_=t1b[:, 8 * 65:16 * 65].rearrange("p (h c) -> p h c", c=65)[:, :, 64:65].rearrange("p h c -> p (h c)"))
                t1n = t1b[:, 8 * 65:16 * 65].rearrange("p (h c) -> p h c", c=65)[:, :, 0:DH]
                dv3 = dinv[:, 8:16].unsqueeze(2).broadcast_to([128, 8, DH])
                ins = e.tensor_tensor(out=attnbf[:, 8 * DH:16 * DH].rearrange("p (h d) -> p h d", h=8),
                                      in0=t1n, in1=dv3, op=ALU.mult)
                P.pub('dve', ins, ('v2attn', g))
                P.wait(e, 'pe', ('p2attnT', g))
                ins = e.tensor_copy(xlnT[:], p0bf[:])
                P.pub('dve', ins, ('v2attnTe', g))
            if 1 <= g <= NT:
                h = g - 1
                P.wait(e, 'act', ('a2xln2', h))
                if h >= 1:
                    P.wait(e, 'pe', ('p2aqT', h - 1))
                e.tensor_reduce(out=amaxb[:], in_=xln2f[:], axis=AX.X, op=ALU.max,
                                apply_absolute_value=True)
                e.tensor_scalar(out=am8[:], in0=amaxb[:], scalar1=1e-8, scalar2=None, op0=ALU.add)
                e.reciprocal(out=rzb[:], in_=am8[:])
                e.tensor_scalar(out=s2b[:], in0=am8[:], scalar1=wsd127[:], scalar2=None, op0=ALU.mult)
                e.tensor_scalar(out=xln2f[:], in0=xln2f[:], scalar1=rzb[:], scalar2=127.0,
                                op0=ALU.mult, op1=ALU.mult)
                e.tensor_scalar(out=xln2f[:], in0=xln2f[:], scalar1=12582912.0,
                                scalar2=-12582912.0, op0=ALU.add, op1=ALU.add)
                e.tensor_copy(aqi[:], xln2f[:])
                ins = e.tensor_copy(aqbf[:], aqi[:])
                P.pub('dve', ins, ('v2aq', h))
            if 1 <= g <= NT:
                h = g - 1
                P.wait(e, 'pe', ('p2aqT', h))
                ins = e.tensor_copy(xlnT[:], p0bf[:])
                P.pub('dve', ins, ('v2aqTe', h))

    def stream_act(e):
        for g in range(-1, NT + 1):
            if 0 <= g < NT:
                P.wait(e, 'pe', ('p1xlnT', g))
                ins = e.activation(out=xlnT[:], in_=p0bf[:], func=AF.Identity)
                P.pub('act', ins, ('v1xlnTe', g))
            if 0 <= g + 1 < NT:
                gn = g + 1
                parn = gn % 2
                P.wait(e, 'dmax%d' % (gn % 3), ('x1', gn))
                if gn == 0:
                    P.wait(e, 'dve', ('ffn2', 7))
                ins = e.activation(out=sqjunk[:], in_=xb[gn % 3][:], func=AF.Square,
                                   accum_out=bnst[parn][:, 0:1])
                P.pub('act', ins, ('a1sq', gn))
                if gn >= 3:
                    P.wait(e, 'pe', ('p1sx', gn - 3))
                ins = ln_apply(e, gn, 'a1xln', 'v1bn', mv, rstd, xlnbf[gn % 3])
                P.pub('act', ins, ('a1xln', gn))
            if 0 <= g < NT:
                P.wait(e, 'pe', ('p1k', g))
                if g >= 1:
                    P.wait(e, 'dve', ('v1diag', g - 1))
                ins = e.activation(out=scrK[:], in_=W1[:], func=AF.Square)
                P.pub('act', ins, ('a1ksq', g))
                P.wait(e, 'pe', ('p1kT', g))
                ins = e.activation(out=kqT2[g % 2][:], in_=p0bf[:], func=AF.Identity)
                P.pub('act', ins, ('v1kTe', g))
                P.wait(e, 'dve', ('v1diag', g))
                ins = e.activation(out=akbuf[:], in_=diag[:], func=AF.Exp, scale=-C2H)
                P.pub('act', ins, ('a1ak', g))
            if 1 <= g <= NT:
                h = g - 1
                for b in range(4):
                    P.wait(e, 'pe', ('p1dd%d' % b, h))
                    if h >= 1:
                        P.wait(e, 'dve', ('v1gm%d' % b, h - 1))
                    win = (W1 if b % 2 == 0 else W2)
                    ins = e.activation(out=expbuf[:, 1024 * b:1024 * (b + 1)], in_=win[:],
                                       func=AF.Exp)
                    P.pub('act', ins, ('a1ek%d' % b, h))
        for g in range(-1, NT + 2):
            if 0 <= g < NT and ttof(g) == 0:
                sq = seqof(g)
                P.wait(e, 'pe', ('bscolbc', sq))
                ins = e.activation(out=bscol[:].rearrange("p (b q) -> p b q", q=455),
                                   in_=pall[:, 0:1536].rearrange("p (b q) -> p b q", q=512)[:, :, 0:455],
                                   func=AF.Identity, scale=B_EPS)
                P.pub('act', ins, ('abscol', sq))
            if 2 <= g <= NT + 1:
                hm = g - 2
                P.wait(e, 'dve', ('v2out', hm))
                r0 = seqof(hm) * S + ttof(hm) * T
                ins = e.dma_start(out=out_hbm[r0:r0 + T, :], in_=xb[hm % 3][:])
                P.pub('dmao%d' % (hm % 3), ins, ('o', hm), 16)
            if 0 <= g < NT:
                P.wait(e, 'pe', ('p2q', g))
                ins = e.activation(out=kqbf[:], in_=W1[:], func=AF.Identity)
                P.pub('act', ins, ('v2qe', g))
                if g >= 1:
                    P.wait(e, 'dve', ('v2diagq', g - 1))
                ins = e.activation(out=scrK[:], in_=W1[:], func=AF.Square)
                P.pub('act', ins, ('a2qsq', g))
                P.wait(e, 'pe', ('p2qT', g))
                ins = e.activation(out=kqT2[g % 2][:], in_=p0bf[:], func=AF.Identity)
                P.pub('act', ins, ('v2qTe', g))
            if 1 <= g <= NT:
                h = g - 1
                parh = h % 2
                P.wait(e, 'dve', ('v2x1', h))
                P.wait(e, 'dve', ('v2diagq', h))
                e.activation(out=sqjunk[:], in_=xb[h % 3][:], func=AF.Identity,
                             accum_out=bnst2[parh][:, 2:3])
                ins = e.activation(out=sqjunk[:], in_=xb[h % 3][:], func=AF.Square,
                                   accum_out=bnst2[parh][:, 0:1])
                P.pub('act', ins, ('a2lsq', h))
            if 0 <= g < NT:
                for b in range(8):
                    P.wait(e, 'pe', ('p2ddmm%d' % b, g))
                    win = pall[:, 2048 + 512 * (b % 3):2048 + 512 * (b % 3) + 512]
                    ins = e.activation(out=expbuf[:, 512 * b:512 * (b + 1)], in_=win,
                                       func=AF.Exp)
                    P.pub('act', ins, ('a2g%d' % b, g))
                    if b == 5:
                        P.wait(e, 'dve', ('v2dr', g))
                        ins = e.activation(out=aexp[:], in_=drb[:], func=AF.Exp, scale=-1.0)
                        P.pub('act', ins, ('a2aexp', g))
            if 1 <= g <= NT:
                h = g - 1
                parh = h % 2
                if h >= 1:
                    P.wait(e, 'dve', ('v2aq', h - 1))
                P.wait(e, 'dve', ('v2lbn', h))
                e.activation(out=tmp1[:], in_=mv2[parh][:, 1:2], func=AF.Ln, bias=lnepsb[:])
                e.activation(out=rstd2[parh][:], in_=tmp1[:], func=AF.Exp, scale=-0.5)
                e.activation(out=nmr[:], in_=mv2[parh][:, 0:1], func=AF.Identity,
                             scale=rstd2[parh][:])
                ins = e.activation(out=xln2f[:], in_=xb[h % 3][:], func=AF.Identity,
                                   scale=rstd2[parh][:], bias=nmr[:])
                P.pub('act', ins, ('a2xln2', h))
            if 0 <= g + 1 < NT:
                gn = g + 1
                parn = gn % 2
                P.wait(e, 'dmax%d' % (gn % 3), ('x2', gn))
                if gn >= 2:
                    P.wait(e, 'pe', ('p2xlnT', gn - 2))
                P.wait(e, 'dve', ('v1mvs', gn))
                e.activation(out=tmp1[:], in_=mvsave[:, 2 * gn + 1:2 * gn + 2],
                             func=AF.Ln, bias=lnepsb[:])
                e.activation(out=rstd[parn][:], in_=tmp1[:], func=AF.Exp, scale=-0.5)
                e.activation(out=nmr[:], in_=mvsave[:, 2 * gn:2 * gn + 1],
                             func=AF.Identity, scale=rstd[parn][:])
                ins = e.activation(out=xlnbf[gn % 3][:], in_=xb[gn % 3][:],
                                   func=AF.Identity, scale=rstd[parn][:], bias=nmr[:])
                P.pub('act', ins, ('a2xln', gn))
            if 0 <= g + 1 < NT:
                gn = g + 1
                P.wait(e, 'pe', ('p2xlnT', gn))
                ins = e.activation(out=xlnT2[:], in_=p1bf[:], func=AF.Identity)
                P.pub('act', ins, ('v2xlnTe', gn))
        if not P.pass0:
            for pp in range(3):
                cnt = len([x for x in range(NT) if x % 3 == pp])
                e.wait_ge(P.sem('dmao%d' % pp), 16 * cnt)

    def stream_pe(e):
        P.wait(e, 'gp', 'ident')
        P.wait(e, 'dve', 'abssum')
        ins = e.transpose(out=pall[0:1, 0:128], in_=abssum[:], identity=idf32[:])
        P.pub('pe', ins, 'wsT')
        P.wait(e, 'dve', 'wsrow')
        ins = e.matmul(pall[:, 512:513], onesrow[:], wsrow[:], start=True, stop=True)
        P.pub('pe', ins, 'wsbc')
        for g in range(NT + 1):
            par = g % 2
            if g < NT:
                P.wait(e, 'act', ('a1xln', g))
                if g >= 1:
                    P.wait(e, 'act', ('v1kTe', g - 1))
                ins = None
                for c in range(8):
                    ins = e.transpose(out=p0bf[:, 128 * c:128 * (c + 1)],
                                      in_=xlnbf[g % 3][:, 128 * c:128 * (c + 1)], identity=idbf[:])
                P.pub('pe', ins, ('p1xlnT', g))
                P.wait(e, 'act', ('v1xlnTe', g))
                if g >= 2:
                    P.wait(e, 'act', ('a1ek2', g - 2))
                    P.wait(e, 'dve', ('v1sx', g - 2))
                elif g == 1:
                    P.wait(e, 'dve', ('v1ke', 0))
                if g == 0:
                    P.wait(e, 'dmawk', ('wk', 7))
                ins = None
                for j in range(2):
                    for c in range(8):
                        ins = e.matmul(W1[:, 512 * j:512 * (j + 1)],
                                       xlnT[:, 128 * c:128 * (c + 1)],
                                       WA[:, D * c + 512 * j:D * c + 512 * (j + 1)],
                                       start=(c == 0), stop=(c == 7))
                P.pub('pe', ins, ('p1k', g))
                if g >= 2:
                    P.wait(e, 'act', ('a1ek3', g - 2))
                elif g == 1:
                    P.wait(e, 'dve', ('v1v1', 0))
                if g == 0:
                    P.wait(e, 'dmawv', ('wv', 7))
                ins = None
                for j in range(2):
                    for c in range(8):
                        ins = e.matmul(W2[:, 512 * j:512 * (j + 1)],
                                       xlnT[:, 128 * c:128 * (c + 1)],
                                       WB[:, D * c + 512 * j:D * c + 512 * (j + 1)],
                                       start=(c == 0), stop=(c == 7))
                P.pub('pe', ins, ('p1v', g))
                P.wait(e, 'dve', ('v1ke', g))
                ins = None
                for hh in range(8):
                    ins = e.transpose(out=p0bf[0:64, 128 * hh:128 * (hh + 1)],
                                      in_=kqbf[:, 64 * hh:64 * (hh + 1)], identity=idbf[:])
                for hh in range(8):
                    ins = e.transpose(out=p0bf[64:128, 128 * hh:128 * (hh + 1)],
                                      in_=kqbf[:, 64 * (hh + 8):64 * (hh + 9)], identity=idbf[:])
                P.pub('pe', ins, ('p1kT', g))
            if g >= 1:
                h = g - 1
                parh = h % 2
                sqh, tth = seqof(h), ttof(h)
                P.wait(e, 'dve', 'pjscale')
                for b in range(4):
                    P.wait(e, 'act', ('v1kTe', h))
                    win = (W1 if b % 2 == 0 else W2)
                    if b == 0:
                        if g < NT:
                            P.wait(e, 'dve', ('v1ke', g))
                        else:
                            P.wait(e, 'act', ('a1ek2', g - 2))
                            P.wait(e, 'dve', ('v1sx', g - 2))
                    elif b == 1:
                        if g < NT:
                            P.wait(e, 'dve', ('v1v1', g))
                        else:
                            P.wait(e, 'act', ('a1ek3', g - 2))
                    else:
                        P.wait(e, 'act', ('a1ek%d' % (b - 2), h))
                    ins = None
                    for hd in (2 * b, 2 * b + 8, 2 * b + 1, 2 * b + 9):
                        r0_, r1_ = kqrows(hd)
                        col = expcol(hd) - 1024 * b
                        ins = e.matmul(win[:, col:col + 256],
                                       kqT2[parh][r0_:r1_, kqcol(hd):kqcol(hd) + 128],
                                       projTb[r0_:r1_, :], start=True, stop=True,
                                       tile_position=(r0_, 0))
                    P.pub('pe', ins, ('p1dd%d' % b, h))
                for half, nm in ((0, 'A'), (1, 'B')):
                    P.wait(e, 'act', ('a1ek3', h))
                    if half == 0 and h >= 1:
                        P.wait(e, 'dve', ('v1c0b', h - 1))
                    if half == 1:
                        P.wait(e, 'dve', ('v1c0a', h))
                    ins = None
                    for hh in range(8):
                        hd = 8 * half + hh
                        for mc in range(2):
                            s = mc * 8 + hh
                            ins = e.matmul(pall[:, c0_slot_psum(s):c0_slot_psum(s) + 65],
                                           expbuf[:, expcol(hd) + 128 * mc:expcol(hd) + 128 * mc + 128],
                                           v1buf2[parh][:, 65 * hd:65 * hd + 65],
                                           start=True, stop=True)
                    P.pub('pe', ins, ('p1c0%s' % nm, h))
                P.wait(e, 'act', ('a1ek2', h))
                ins = None
                for j in range(2):
                    ins = e.matmul(pall[0:1, 512 * j:512 * (j + 1)], onescol[:],
                                   xlnbf[h % 3][:, 512 * j:512 * (j + 1)], start=True, stop=True)
                P.pub('pe', ins, ('p1sx', h))
        P.wait(e, 'dve', ('v1gm3', NT - 1))
        P.wait(e, 'dve', ('v1sx', NT - 1))
        ins = e.transpose(out=pall[0:1, 0:128], in_=gmax128[:], identity=idf32[:])
        P.pub('pe', ins, 'gmaxT')
        P.wait(e, 'dmamsc', 'msc')
        ins = e.matmul(pall[:, 512:513], onesrow[:], Msc[0:1, 0:1], start=True, stop=True)
        P.pub('pe', ins, 'mbc')
        for s in range(nseq):
            P.wait(e, 'dve', 'gmaxfin')
            P.wait(e, 'dve', 'aeM')
            if s > 0:
                P.wait(e, 'dve', ('vsumv', s - 1))
            ins = None
            for c in range(8):
                ins = e.transpose(out=pall[:, c:c + 1],
                                  in_=sxl[0:1, s * D + 128 * c:s * D + 128 * (c + 1)],
                                  identity=onesrow[0:1, 0:1])
            P.pub('pe', ins, ('sxlT', s))
            P.wait(e, 'dve', ('vsxlT', s))
            ins = None
            for j in range(2):
                for c in range(8):
                    ins = e.matmul(pall[0:1, 1024 + 512 * j:1024 + 512 * (j + 1)],
                                   sxlT[:, c:c + 1],
                                   WB[:, D * c + 512 * j:D * c + 512 * (j + 1)],
                                   start=(c == 0), stop=(c == 7))
            P.pub('pe', ins, ('sumv', s))
        for g in range(NT + 1):
            par = g % 2
            if g < NT:
                sq, tt = seqof(g), ttof(g)
                if tt == 0:
                    for grp in range(2):
                        P.wait(e, 'dve', ('vsumv', nseq - 1))
                        P.wait(e, 'dve', 'aeM')
                        if grp == 1:
                            P.wait(e, 'dve', ('vctx0', sq))
                        elif g >= 1:
                            P.wait(e, 'dve', ('v2t1', g - 1))
                            P.wait(e, 'dve', ('v2x1', g - 1))
                            if g >= 2:
                                P.wait(e, 'dve', ('v2out', g - 2))
                        ins = None
                        for hh in range(8):
                            hd = 8 * grp + hh
                            for mc in range(2):
                                s = mc * 8 + hh
                                col = (s // 7) * 512 + (s % 7) * 65
                                ins = e.matmul(pall[:, col:col + 64], onesrow[:],
                                               rows4[0:1, sq * D + DH * hd:sq * D + DH * (hd + 1)],
                                               start=True, stop=True)
                        P.pub('pe', ins, ('ub', sq, grp))
                    P.wait(e, 'dve', ('vctx1', sq))
                    P.wait(e, 'dve', ('vctx', sq))
                    ins = None
                    for (d0, s0, n) in ((0, 0, 512), (512, 512, 512), (1024, 1024, 16),
                                        (1040, 0, 496), (1536, 496, 512), (2048, 1008, 32)):
                        srccol = (1040 + s0) if d0 >= 1040 else s0
                        ins = e.matmul(pall[0:1, d0:d0 + n], onescol[:],
                                       ctxaug[:, srccol:srccol + n], start=True, stop=True)
                    P.pub('pe', ins, ('scol', sq))
                    P.wait(e, 'dve', ('vscol', sq))
                    ins = None
                    bse = nseq * D
                    for (bk, n) in ((0, 455), (1, 455), (2, 130)):
                        ins = e.matmul(pall[:, 512 * bk:512 * bk + n], onesrow[:],
                                       rows4[0:1, bse + 455 * bk:bse + 455 * bk + n],
                                       start=True, stop=True)
                    P.pub('pe', ins, ('bscolbc', sq))
                if g == 0:
                    P.wait(e, 'act', ('a2xln', 0))
                    P.wait(e, 'dve', ('v1c0b', NT - 1))
                    ins = None
                    for c in range(8):
                        ins = e.transpose(out=p1bf[:, 128 * c:128 * (c + 1)],
                                          in_=xlnbf[0][:, 128 * c:128 * (c + 1)],
                                          identity=idbf[:])
                    P.pub('pe', ins, ('p2xlnT', 0))
                P.wait(e, 'act', ('v2xlnTe', g))
                P.wait(e, 'act', ('abscol', sq))
                if g >= 1:
                    P.wait(e, 'dve', ('v2x1', g - 1))
                if g == 0:
                    P.wait(e, 'dmawq', ('wq', 7))
                ins = None
                for j in range(2):
                    for c in range(8):
                        ins = e.matmul(W1[:, 512 * j:512 * (j + 1)],
                                       xlnT2[:, 128 * c:128 * (c + 1)],
                                       WA[:, D * c + 512 * j:D * c + 512 * (j + 1)],
                                       start=(c == 0), stop=(c == 7))
                P.pub('pe', ins, ('p2q', g))
                P.wait(e, 'act', ('v2qe', g))
                ins = None
                for hh in range(8):
                    ins = e.transpose(out=p0bf[0:64, 128 * hh:128 * (hh + 1)],
                                      in_=kqbf[:, 64 * hh:64 * (hh + 1)], identity=idbf[:])
                for hh in range(8):
                    ins = e.transpose(out=p0bf[64:128, 128 * hh:128 * (hh + 1)],
                                      in_=kqbf[:, 64 * (hh + 8):64 * (hh + 9)], identity=idbf[:])
                P.pub('pe', ins, ('p2qT', g))
                for b in range(4):
                    P.wait(e, 'act', ('v2qTe', g))
                    win = (W1 if b % 2 == 0 else W2)
                    if b == 0:
                        P.wait(e, 'act', ('v2qe', g))
                        if tt == 0:
                            P.wait(e, 'act', ('abscol', sq))
                    elif b == 1:
                        if tt == 0:
                            P.wait(e, 'dve', ('vscol', sq))
                            P.wait(e, 'act', ('abscol', sq))
                        if g >= 2:
                            P.wait(e, 'dve', ('v2out', g - 2))
                    else:
                        P.wait(e, 'dve', ('v2rm%d' % (b - 2), g))
                    ins = None
                    for hd in (2 * b, 2 * b + 8, 2 * b + 1, 2 * b + 9):
                        r0_, r1_ = kqrows(hd)
                        col = expcol(hd) - 1024 * b
                        ins = e.matmul(win[:, col:col + 256],
                                       kqT2[par][r0_:r1_, kqcol(hd):kqcol(hd) + 128],
                                       projTb[r0_:r1_, :], start=True, stop=True,
                                       tile_position=(r0_, 0))
                    P.pub('pe', ins, ('p2ddtm%d' % b, g))
                for b in range(8):
                    P.wait(e, 'act', ('v2qTe', g))
                    if b == 0:
                        if tt == 0:
                            P.wait(e, 'dve', ('vscol', sq))
                        if g >= 1:
                            P.wait(e, 'dve', ('v2t1', g - 1))
                    if b < 3:
                        if g >= 1:
                            P.wait(e, 'act', ('a2g%d' % {0: 6, 1: 7, 2: 5}[b], g - 1))
                    else:
                        P.wait(e, 'act', ('a2g%d' % (b - 3), g))
                    win = pall[:, 2048 + 512 * (b % 3):2048 + 512 * (b % 3) + 512]
                    ins = None
                    for j in range(2):
                        hd = 2 * b + j
                        r0_, r1_ = kqrows(hd)
                        for mc in range(2):
                            ins = e.matmul(win[:, 256 * j + 128 * mc:256 * j + 128 * mc + 128],
                                           projTb[r0_:r1_, 128 * mc:128 * (mc + 1)],
                                           kqT2[par][r0_:r1_, kqcol(hd):kqcol(hd) + 128],
                                           start=True, stop=True, tile_position=(r0_, 0))
                    P.pub('pe', ins, ('p2ddmm%d' % b, g))
                P.wait(e, 'dve', ('vctx', sq))
                for (s0, ns) in GC_PIECES:
                    P.wait(e, 'act', ('a2g%d' % (6 if s0 == 0 else 7), g))
                    ins = None
                    for hd in range(s0, s0 + ns):
                        grp = hd // 8
                        gcb = 512 * (hd // 2) + 256 * (hd % 2)
                        for mc in range(2):
                            s = mc * 8 + (hd % 8)
                            ins = e.matmul(pall[:, gc_col(hd):gc_col(hd) + 65],
                                           expbuf[:, gcb + 128 * mc:gcb + 128 * mc + 128],
                                           ctxaug[:, grp * 1040 + s * 65:grp * 1040 + s * 65 + 65],
                                           start=(mc == 0), stop=(mc == 1))
                    P.pub('pe', ins, ('p2gc%d' % s0, g))
            if g < NT:
                if g == 0:
                    P.wait(e, 'dmawo', ('wo', 7))
                    P.wait(e, 'dmabo', 'bo')
                P.wait(e, 'dve', ('v2rm2', g))
                for j in range(2):
                    e.matmul(W1[:, 512 * j:512 * (j + 1)], onesrow[:],
                             bo_row[0:1, 512 * j:512 * (j + 1)], start=True, stop=False)
                P.wait(e, 'dve', ('v2attnA', g))
                ins = None
                for c in range(4):
                    ins = e.transpose(out=p0bf[:, 128 * c:128 * (c + 1)],
                                      in_=attnbf[:, 128 * c:128 * (c + 1)], identity=idbf[:])
                P.wait(e, 'dve', ('v2attn', g))
                for c in range(4, 8):
                    ins = e.transpose(out=p0bf[:, 128 * c:128 * (c + 1)],
                                      in_=attnbf[:, 128 * c:128 * (c + 1)], identity=idbf[:])
                P.pub('pe', ins, ('p2attnT', g))
                if g + 1 < NT:
                    P.wait(e, 'act', ('a2xln', g + 1))
                    P.wait(e, 'dve', ('v2t1', g))
                    ins = None
                    for c in range(8):
                        ins = e.transpose(out=p1bf[:, 128 * c:128 * (c + 1)],
                                          in_=xlnbf[(g + 1) % 3][:, 128 * c:128 * (c + 1)],
                                          identity=idbf[:])
                    P.pub('pe', ins, ('p2xlnT', g + 1))
                P.wait(e, 'dve', ('v2attnTe', g))
                ins = None
                for j in range(2):
                    for c in range(8):
                        ins = e.matmul(W1[:, 512 * j:512 * (j + 1)],
                                       xlnT[:, 128 * c:128 * (c + 1)],
                                       WB[:, D * c + 512 * j:D * c + 512 * (j + 1)],
                                       start=False, stop=(c == 7))
                P.pub('pe', ins, ('p2y', g))
            if g >= 1:
                h = g - 1
                P.wait(e, 'dve', ('v2aq', h))
                if g == NT:
                    P.wait(e, 'dve', ('v2attnTe', h))
                ins = None
                for c in range(8):
                    ins = e.transpose(out=p0bf[:, 128 * c:128 * (c + 1)],
                                      in_=aqbf[:, 128 * c:128 * (c + 1)], identity=idbf[:])
                P.pub('pe', ins, ('p2aqT', h))
                P.wait(e, 'dve', ('v2aqTe', h))
                if g < NT:
                    P.wait(e, 'dve', ('v2rm3', g))
                elif h >= 1:
                    P.wait(e, 'dve', ('v2out', h - 1))
                ins = None
                for j in range(2):
                    for c in range(8):
                        ins = e.matmul(W2[:, 512 * j:512 * (j + 1)],
                                       xlnT[:, 128 * c:128 * (c + 1)],
                                       WC[:, D * c + 512 * j:D * c + 512 * (j + 1)],
                                       start=(c == 0), stop=(c == 7))
                P.pub('pe', ins, ('p2y2', h))

    streams = [stream_sync, stream_gp, stream_dve, stream_act, stream_pe]

    sch.begin(True)
    fake = _FakeEng()
    for fn in streams:
        fn(fake)

    sch.begin(False)
    with nc.Block() as block:
        @block.sync
        def _(eng):
            stream_sync(eng)

        @block.gpsimd
        def _(eng):
            stream_gp(eng)

        @block.vector
        def _(eng):
            stream_dve(_DrainEng(eng))

        @block.scalar
        def _(eng):
            stream_act(_DrainEng(eng))

        @block.tensor
        def _(eng):
            stream_pe(eng)

    return nc


def _run(x, wq, wk, wv, wo, bo, proj, wffn, nseq, ntiles, cores):
    S = ntiles * T
    rows = nseq * S
    ncores = len(cores)
    xs = np.ascontiguousarray(np.asarray(x, np.float32)).reshape(ncores * nseq, S, D)
    wfT = np.ascontiguousarray(np.asarray(wffn, np.float32).T)
    pjT = np.ascontiguousarray(np.asarray(proj, np.float32).T)
    bo2 = np.ascontiguousarray(np.asarray(bo, np.float32).reshape(1, D))
    wq = np.ascontiguousarray(np.asarray(wq, np.float32))
    wk = np.ascontiguousarray(np.asarray(wk, np.float32))
    wv = np.ascontiguousarray(np.asarray(wv, np.float32))
    wo = np.ascontiguousarray(np.asarray(wo, np.float32))
    nc = build_program(nseq, ntiles)
    in_maps = []
    for c in range(ncores):
        shard = np.ascontiguousarray(xs[c * nseq:(c + 1) * nseq].reshape(rows, D))
        in_maps.append({"x": shard, "wq": wq, "wk": wk, "wv": wv, "wo": wo,
                        "bo": bo2, "projT": pjT, "wfT": wfT})
    res = run_bass_kernel_spmd(nc, in_maps, core_ids=cores)
    outs = [res.results[c]["out"].reshape(nseq, S, D) for c in range(ncores)]
    return np.concatenate(outs, axis=0)


def kernel(L0_attn_x, L0_attn_wq, L0_attn_wk, L0_attn_wv, L0_attn_wo,
           L0_attn_bo, L0_attn_proj, L0_ffn_weight):
    out = _run(L0_attn_x, L0_attn_wq, L0_attn_wk, L0_attn_wv, L0_attn_wo,
               L0_attn_bo, L0_attn_proj, L0_ffn_weight,
               nseq=4, ntiles=16, cores=list(range(8)))
    return out.reshape(32, 2048, D)
